# version 15
# baseline (speedup 1.0000x reference)
import numpy as np
import ml_dtypes
from contextlib import ExitStack
import concourse.bass as bass
import concourse.mybir as mybir
from concourse.bass_utils import run_bass_kernel_spmd
from concourse.alu_op_type import AluOpType as OP

F32 = mybir.dt.float32
F32R = mybir.dt.float32r
BF16 = mybir.dt.bfloat16
AF = mybir.ActivationFunctionType
AX = mybir.AxisListType

S, E, H, HD = 4096, 1024, 16, 64
NTC = S // 128
NEC = E // 128
SHIFT = 12.0
EPS = 1e-5
NCORES = 8
VW = 65
QB = 512
NQB = S // QB
NKC = NTC


class Sem:
    def __init__(self, h):
        self.h = h
        self.n = 0

    def inc(self, k=1):
        self.n += k
        return self.n


def build_nc(phases="BGPDE", dbg=False):
    nc = bass.Bass("TRN2", target_bir_lowering=False, debug=False)
    DT = nc.dram_tensor
    x_d = DT("x", [S, E], F32, kind="ExternalInput").ap()
    wq_d = DT("wqT", [E, E], F32, kind="ExternalInput").ap()
    wk_d = DT("wkT", [E, E], F32, kind="ExternalInput").ap()
    wv_d = DT("wvT", [E, E], F32, kind="ExternalInput").ap()
    wo_d = DT("woT", [E, E], F32, kind="ExternalInput").ap()
    bq_d = DT("bq", [1, E], F32, kind="ExternalInput").ap()
    bk_d = DT("bk", [1, E], F32, kind="ExternalInput").ap()
    bv_d = DT("bv", [1, E], F32, kind="ExternalInput").ap()
    bo_d = DT("bo", [1, E], F32, kind="ExternalInput").ap()
    bf_d = DT("bffn", [1, E], F32, kind="ExternalInput").ap()
    wr_d = DT("wfr", [E, E], F32, kind="ExternalInput").ap()
    wi_d = DT("wfi", [E, E], F32, kind="ExternalInput").ap()
    ch_d = DT("ch", [E, E], BF16, kind="ExternalInput").ap()
    cl_d = DT("cl", [E, E], BF16, kind="ExternalInput").ap()
    sh_d = DT("sh", [E, E], BF16, kind="ExternalInput").ap()
    sl_d = DT("sl", [E, E], BF16, kind="ExternalInput").ap()
    ct_d = DT("cost", [S, E], F32, kind="ExternalInput").ap()
    st_d = DT("sint", [S, E], F32, kind="ExternalInput").ap()
    id_d = DT("ident", [128, 128], F32, kind="ExternalInput").ap()
    out_d = DT("out", [S, E], F32, kind="ExternalOutput").ap()
    KI = "ExternalOutput" if dbg else "Internal"
    qr_d = DT("qrot", [S, E], BF16, kind=KI).ap()
    kr_d = DT("krot", [S, E], BF16, kind=KI).ap()
    pa_d = DT("partial", [S, E], F32, kind=KI).ap()
    ao_d = DT("attnOT", [8, 128, S], BF16, kind=KI).ap()
    v_d = DT("vaug", [S, 16 * VW], BF16, kind=KI).ap()
    rs_d = DT("rowsums", [64, 1024], BF16, kind="Internal").ap()
    rsr_d = DT("rowrecip", [64, 1024], BF16, kind="Internal").ap()
    md_d = DT("madump", [E, E], F32, kind="ExternalOutput").ap() if dbg else None
    xnd_d = DT("xndump", [S, E], F32, kind="ExternalOutput").ap() if dbg else None
    xtd_d = DT("xtdump", [NTC, 128, E], BF16, kind="ExternalOutput").ap() if dbg else None

    with ExitStack() as stk:
        EN = stk.enter_context
        ar1 = EN(nc.sbuf_tensor("ar1", [128, 16384], F32))
        ar2 = EN(nc.sbuf_tensor("ar2", [128, 16384], F32))
        m_r = EN(nc.sbuf_tensor("m_r", [128, 8192], F32R))
        xnt_r = EN(nc.sbuf_tensor("xnt_r", [128, 2048], F32R))
        vsl = EN(nc.sbuf_tensor("vsl", [128, 8320], BF16))
        ar4 = EN(nc.sbuf_tensor("ar4", [128, 3072], F32))
        bias_a = EN(nc.sbuf_tensor("bias_a", [65, 1024], BF16))
        onesb = EN(nc.sbuf_tensor("onesb", [65, 1152], BF16))
        ln_sb = EN(nc.sbuf_tensor("ln_sb", [128, 32], F32))
        ps_a = EN(nc.psum_tensor("ps_a", [128, 2048], F32))
        ps_b = EN(nc.psum_tensor("ps_b", [128, 1024], F32))
        ps_c = EN(nc.psum_tensor("ps_c", [128, 1024], F32))

        sems = {}
        for name in ("sp", "sv", "sa", "sg", "sd_x", "sd_cs", "sd_s", "sd_g",
                     "sd_sp", "sd_pa", "sd_tr", "sd_ao", "sd_pe", "sd_out",
                     "sd_v", "sd_vl", "sd_rs", "sd_rr"):
            sems[name] = Sem(EN(nc.semaphore(name)))
        sp, sv, sa, sg = sems["sp"], sems["sv"], sems["sa"], sems["sg"]
        sd_x, sd_cs, sd_s, sd_g = sems["sd_x"], sems["sd_cs"], sems["sd_s"], sems["sd_g"]
        sd_sp, sd_pa, sd_tr, sd_ao = sems["sd_sp"], sems["sd_pa"], sems["sd_tr"], sems["sd_ao"]
        sd_pe, sd_out = sems["sd_pe"], sems["sd_out"]
        sd_v, sd_vl = sems["sd_v"], sems["sd_vl"]
        sd_rs, sd_rr = sems["sd_rs"], sems["sd_rr"]

        prog = {k: [] for k in ("tensor", "vector", "scalar", "sync", "gpsimd")}

        def em(eng, fn, waits=(), inc=None):
            if inc is not None:
                inc[0].inc(inc[1])
            prog[eng].append((tuple(waits), fn, inc))

        def lp(f):
            def g(e):
                with nc.allow_low_precision(reason="intentional bf16/f32r rounding"):
                    return f(e)
            return g

        ar1_bf = ar1[:].bitcast(BF16)
        ar2_bf = ar2[:].bitcast(BF16)

        x_v = lambda t: ar1[:, (t % 2) * E: (t % 2) * E + E]
        xn_v = lambda t: ar1[:, 2048 + (t % 2) * E: 2048 + (t % 2) * E + E]
        cs_v = lambda t: ar1[:, 4096 + (t % 2) * E: 4096 + (t % 2) * E + E]
        sn_v = lambda t: ar1[:, 6144 + (t % 2) * E: 6144 + (t % 2) * E + E]
        pp_v = lambda t: ar1[:, 8192 + (t % 2) * E: 8192 + (t % 2) * E + E]
        xtb_v = lambda t: ar1_bf[:, 20480 + (t % 2) * E: 20480 + (t % 2) * E + E]
        rq_v = lambda t: ar1_bf[:, 22528 + (t % 2) * E: 22528 + (t % 2) * E + E]
        rk_v = lambda t: ar1_bf[:, 24576 + (t % 2) * E: 24576 + (t % 2) * E + E]
        ru_v = ar1_bf[:, 26624:27136]
        rt_v = ar1_bf[:, 27136:27648]
        xtr_v = lambda t: xnt_r[:, (t % 2) * E: (t % 2) * E + E]
        pa_e = lambda t: ar1[:, 4096 + (t % 4) * 1024: 4096 + (t % 4) * 1024 + 1024]
        ou_e = lambda t: ar1[:, 13312 + (t % 2) * 1024: 13312 + (t % 2) * 1024 + 1024]

        stgin = ar4[:, 0:1024]
        stg2 = ar4[:, 1536:2560]
        stg_i = lambda i: (stgin, stg2)[i % 2]
        scrap = ar4[:, 0:1024]
        tmp1 = ar4[:, 1024:2048]
        tA = ar4[:, 1024:1152]
        tB = ar4[:, 1152:1280]
        tC = ar4[:, 1280:1408]
        m_part = ar4[:, 1536:2560]
        id_sb = ar4[:, 2560:2688]
        ex_bf = ar4[:, 0:1536].bitcast(BF16)
        as_v = lambda i: ar4[:, 1536 + (i % 2) * 256: 1536 + (i % 2) * 256 + 256].bitcast(BF16)
        rs_v = lambda i: ar4[:, 2048 + (i % 2) * 512: 2048 + (i % 2) * 512 + 512].bitcast(BF16)
        vst = ar4[:, 1024:1544].bitcast(BF16)

        chf = lambda jc, oh: ar1_bf[:, jc * 1024 + oh * 512: jc * 1024 + oh * 512 + 512]
        clf = lambda jc, oh: ar1_bf[:, 8192 + jc * 1024 + oh * 512: 8192 + jc * 1024 + oh * 512 + 512]
        shf = lambda jc, oh: ar1_bf[:, 16384 + jc * 1024 + oh * 512: 16384 + jc * 1024 + oh * 512 + 512]
        slf = lambda jc, oh: ar1_bf[:, 24576 + jc * 1024 + oh * 512: 24576 + jc * 1024 + oh * 512 + 512]
        wrh = lambda kc, ac: ar2_bf[:, kc * 1024 + ac * 128: kc * 1024 + ac * 128 + 128]
        wrl = lambda kc, ac: ar2_bf[:, 8192 + kc * 1024 + ac * 128: 8192 + kc * 1024 + ac * 128 + 128]
        wih = lambda kc, ac: ar2_bf[:, 16384 + kc * 1024 + ac * 128: 16384 + kc * 1024 + ac * 128 + 128]
        wil = lambda kc, ac: ar2_bf[:, 24576 + kc * 1024 + ac * 128: 24576 + kc * 1024 + ac * 128 + 128]
        t1h = vsl[:, 0:1024]
        t1l = vsl[:, 1024:2048]
        t2h = vsl[:, 2048:3072]
        t2l = vsl[:, 3072:4096]
        rsl = lambda base, kc, rc: ar1_bf[:, base + kc * 1024 + rc * 128: base + kc * 1024 + rc * 128 + 128]
        vbuf = lambda i: vsl[:, (i % 2) * 4160: (i % 2) * 4160 + 4160]

        em("gpsimd", lambda e: e.memset(onesb[:], 1.0), inc=(sg, 1))
        em("gpsimd", lambda e: e.memset(ln_sb[:, 16:17], EPS), inc=(sg, 1))
        em("gpsimd", lambda e: e.memset(ln_sb[:, 24:25], -SHIFT), inc=(sg, 1))
        em("sync", lambda e: e.dma_start(out=id_sb, in_=id_d), inc=(sd_s, 16))
        em("gpsimd", lambda e: e.dma_start(out=bias_a[0:1, :], in_=bq_d), inc=(sd_g, 16))
        em("gpsimd", lambda e: e.dma_start(out=bias_a[32:33, :], in_=bk_d), inc=(sd_g, 16))
        em("gpsimd", lambda e: e.dma_start(out=bias_a[64:65, :], in_=bv_d), inc=(sd_g, 16))
        em("sync", lambda e: e.dma_start(out=tmp1[0:1, :], in_=bo_d), inc=(sd_s, 16))
        em("sync", lambda e: e.dma_start(out=scrap[0:1, :], in_=bf_d), inc=(sd_s, 16))
        em("vector", lp(lambda e: e.tensor_tensor(onesb[0:1, 128:1152], tmp1[0:1, :], scrap[0:1, :], OP.add)),
           waits=[(sd_s, sd_s.n), (sg, 1)], inc=(sv, 1))
        sv_bias = sv.n

        for src, off in ((ch_d, 0), (cl_d, 8192), (sh_d, 16384), (sl_d, 24576)):
            em("sync", lambda e, src=src, off=off: e.dma_start(
                out=ar1_bf[:, off:off + 8192].rearrange("p (c f) -> p c f", f=1024),
                in_=src.rearrange("(c p) f -> p c f", p=128)), inc=(sd_s, 16))
        sd_s_b = sd_s.n
        wsplit_sv = [sv_bias]
        wi_ctr = 0
        for i, (wsrc, offh, offl) in enumerate(((wr_d, 0, 8192), (wi_d, 16384, 24576))):
            for c in range(NEC):
                j = wi_ctr
                wi_ctr += 1
                stg = stg_i(j)
                w = [(sv, sv_bias)] if j < 2 else [(sv, wsplit_sv[j - 1])]
                em("gpsimd", lambda e, wsrc=wsrc, c=c, stg=stg: e.dma_start(
                    out=stg, in_=wsrc[c * 128:(c + 1) * 128, :]), waits=w, inc=(sd_g, 16))
                hi = ar2_bf[:, offh + c * 1024: offh + c * 1024 + 1024]
                lo = ar2_bf[:, offl + c * 1024: offl + c * 1024 + 1024]
                em("vector", lp(lambda e, hi=hi, stg=stg: e.tensor_copy(hi, stg)),
                   waits=[(sd_g, sd_g.n)], inc=(sv, 1))
                wtmp = vsl[:].bitcast(F32)[:, 0:1024]
                em("vector", lambda e, hi=hi, wtmp=wtmp: e.tensor_copy(wtmp, hi),
                   waits=[(sv, sv.n)], inc=(sv, 1))
                em("vector", lp(lambda e, lo=lo, stg=stg, wtmp=wtmp: e.tensor_tensor(
                    lo, stg, wtmp, OP.subtract)),
                   waits=[(sv, sv.n)], inc=(sv, 1))
                wsplit_sv.append(sv.n)
        sv_wsplit = sv.n

        CH, CL, SH, SL = 0, 8192, 16384, 24576
        st1a_pairs = ((wrh, CH), (wrl, CH), (wrh, CL), (wih, SH), (wil, SH), (wih, SL))
        g1 = ((wih, CH), (wil, CH), (wih, CL))
        g2 = ((wrh, SH), (wrl, SH), (wrh, SL))
        st2 = ((t1h, chf), (t1l, chf), (t1h, clf), (t2h, shf), (t2l, shf), (t2h, slf))
        t1ev_prev = t2ev_prev = None
        mpart_ev = {}
        dump_dma = 0
        for rc in range(NEC):
            t1ev = []
            for ac in range(NEC):
                bank = ps_a[:, (ac % 2) * 512:(ac % 2) * 512 + 128]
                if ac == 0 and rc == 0:
                    w0 = [(sv, sv_wsplit), (sd_s, sd_s_b)]
                elif ac < 2:
                    w0 = [(sv, t1ev_prev[6 + ac])] if rc else []
                else:
                    w0 = [(sv, t1ev[ac - 2])]
                first = True
                k = 0
                for kc in range(NEC):
                    for (lt, rbase) in st1a_pairs:
                        k += 1
                        last = k == len(st1a_pairs) * NEC
                        em("tensor", lambda e, lt=lt, rb=rbase, kc=kc, ac=ac, rc=rc, b=bank, s0=first, s1=last:
                           e.matmul(b, lt(kc, ac), rsl(rb, kc, rc), start=s0, stop=s1),
                           waits=w0 if first else [], inc=(sp, 1) if last else None)
                        first = False
                        w0 = []
                hs = t1h[:, ac * 128:ac * 128 + 128]
                ls = t1l[:, ac * 128:ac * 128 + 128]
                em("vector", lambda e, b=bank: e.tensor_scalar(tA, b, 1.0 / E, 0.0, OP.mult, OP.bypass),
                   waits=[(sp, sp.n)], inc=(sv, 1))
                em("vector", lp(lambda e, hs=hs: e.tensor_copy(hs, tA)), waits=[(sv, sv.n)], inc=(sv, 1))
                em("vector", lambda e, hs=hs: e.tensor_copy(tB, hs), waits=[(sv, sv.n)], inc=(sv, 1))
                em("vector", lp(lambda e, ls=ls: e.tensor_tensor(ls, tA, tB, OP.subtract)),
                   waits=[(sv, sv.n)], inc=(sv, 1))
                t1ev.append(sv.n)
            t2ev = []
            for ac in range(NEC):
                bA = ps_a[:, 1024 + (ac % 2) * 512: 1024 + (ac % 2) * 512 + 128]
                bB = ps_c[:, (ac % 2) * 512:(ac % 2) * 512 + 128]
                if ac < 2:
                    w0 = [(sv, t2ev_prev[6 + ac])] if rc else []
                else:
                    w0 = [(sv, t2ev[ac - 2])]
                for gi, (bk_, pairs) in enumerate(((bA, g1), (bB, g2))):
                    first = True
                    k = 0
                    for kc in range(NEC):
                        for (lt, rbase) in pairs:
                            k += 1
                            last = k == len(pairs) * NEC
                            em("tensor", lambda e, lt=lt, rb=rbase, kc=kc, ac=ac, rc=rc, b=bk_, s0=first, s1=last:
                               e.matmul(b, lt(kc, ac), rsl(rb, kc, rc), start=s0, stop=s1),
                               waits=w0 if (first and gi == 0) else [],
                               inc=(sp, 1) if (last and gi == 1) else None)
                            first = False
                hs = t2h[:, ac * 128:ac * 128 + 128]
                ls = t2l[:, ac * 128:ac * 128 + 128]
                em("vector", lambda e, b=bA: e.tensor_scalar(tA, b, 1.0 / E, 0.0, OP.mult, OP.bypass),
                   waits=[(sp, sp.n)], inc=(sv, 1))
                em("vector", lambda e, b=bB: e.tensor_scalar(tB, b, 1.0 / E, 0.0, OP.mult, OP.bypass),
                   inc=(sv, 1))
                em("vector", lambda e: e.tensor_tensor(tC, tB, tA, OP.subtract), waits=[(sv, sv.n)], inc=(sv, 1))
                em("vector", lp(lambda e, hs=hs: e.tensor_copy(hs, tC)), waits=[(sv, sv.n)], inc=(sv, 1))
                em("vector", lambda e, hs=hs: e.tensor_copy(tA, hs), waits=[(sv, sv.n)], inc=(sv, 1))
                em("vector", lp(lambda e, ls=ls: e.tensor_tensor(ls, tC, tA, OP.subtract)),
                   waits=[(sv, sv.n)], inc=(sv, 1))
                t2ev.append(sv.n)
            sv_stage1 = sv.n
            for oh in range(2):
                bank = ps_b[:, oh * 512: oh * 512 + 512]
                w0 = [(sv, sv_stage1)] if oh == 0 else []
                if rc:
                    w0 = w0 + [(sv, mpart_ev[(rc - 1, oh)])]
                first = True
                k = 0
                for jc in range(NEC):
                    for (lt, rt) in st2:
                        k += 1
                        last = k == len(st2) * NEC
                        em("tensor", lambda e, lt=lt, rt=rt, jc=jc, oh=oh, b=bank, s0=first, s1=last: e.matmul(
                            b, lt[:, jc * 128:jc * 128 + 128], rt(jc, oh), start=s0, stop=s1),
                            waits=w0 if first else [], inc=(sp, 1) if last else None)
                        first = False
                        w0 = []
                mw = [(sp, sp.n)]
                if dbg and rc:
                    mw.append((sd_s, dump_dma))
                em("vector", lp(lambda e, rc=rc, oh=oh, b=bank: e.tensor_copy(
                    m_r[:, rc * 1024 + oh * 512: rc * 1024 + oh * 512 + 512], b)),
                    waits=mw, inc=(sv, 1))
                if dbg:
                    em("vector", lambda e, oh=oh, b=bank: e.tensor_copy(
                        m_part[:, oh * 512:oh * 512 + 512], b), inc=(sv, 1))
                mpart_ev[(rc, oh)] = sv.n
            if dbg:
                em("sync", lambda e, rc=rc: e.dma_start(out=md_d[rc * 128:(rc + 1) * 128, :], in_=m_part),
                   waits=[(sv, mpart_ev[(rc, 1)])], inc=(sd_s, 16))
                dump_dma = sd_s.n
            t1ev_prev, t2ev_prev = t1ev, t2ev
        sp_B_done = sp.n
        sv_B_done = sv.n

        vst_ones = vst.rearrange("p (n w) -> p n w", w=VW)[:, :, 64:65]
        em("gpsimd", lambda e: e.memset(vst_ones, 1.0), waits=[(sv, sv_B_done), (sp, sp_B_done)], inc=(sg, 1))

        for wdd, off in ((wq_d, 0), (wk_d, 8192), (wv_d, 16384), (wo_d, 24576)):
            em("gpsimd", lambda e, wdd=wdd, off=off: e.dma_start(
                out=ar2_bf[:, off:off + 8192].rearrange("p (c f) -> p c f", f=1024),
                in_=wdd.rearrange("(c p) f -> p c f", p=128)),
                waits=[(sp, sp_B_done)] if off == 0 else [], inc=(sd_g, 16))
        sd_g_wts = sd_g.n
        wq_t = lambda ec, h2: ar2_bf[:, ec * 1024 + h2 * 512: ec * 1024 + h2 * 512 + 512]
        wk_t = lambda ec, h2: ar2_bf[:, 8192 + ec * 1024 + h2 * 512: 8192 + ec * 1024 + h2 * 512 + 512]
        wv_t = lambda ec, h2: ar2_bf[:, 16384 + ec * 1024 + h2 * 512: 16384 + ec * 1024 + h2 * 512 + 512]
        wo_t = lambda c, h2: ar2_bf[:, 24576 + c * 1024 + h2 * 512: 24576 + c * 1024 + h2 * 512 + 512]
        mm_t = lambda ec, h2: m_r[:, ec * 1024 + h2 * 512: ec * 1024 + h2 * 512 + 512]

        if "P" not in phases:
            NTC_ = 0
        else:
            NTC_ = NTC
        tc_dve_done = []
        tc_sa_done = []
        tev = []
        qtr_ev = []

        def emit_loads(tc):
            w = [(sv, tc_dve_done[tc - 2]), (sa, tc_sa_done[tc - 2])] if tc >= 2 else \
                [(sp, sp_B_done), (sv, sv_B_done)]
            em("sync", lambda e, tc=tc: e.dma_start(out=x_v(tc), in_=x_d[tc * 128:(tc + 1) * 128, :]),
               waits=w, inc=(sd_x, 16))
            em("sync", lambda e, tc=tc: e.dma_start(out=cs_v(tc), in_=ct_d[tc * 128:(tc + 1) * 128, :]),
               inc=(sd_cs, 16))
            em("sync", lambda e, tc=tc: e.dma_start(out=sn_v(tc), in_=st_d[tc * 128:(tc + 1) * 128, :]),
               inc=(sd_cs, 16))

        def emit_lnT(t):
            xv, xnv = x_v(t), xn_v(t)
            xtb, xtr = xtb_v(t), xtr_v(t)
            em("vector", lambda e, xv=xv: e.tensor_reduce(ln_sb[:, 0:1], xv, AX.X, OP.add),
               waits=[(sd_x, 16 * (t + 1))], inc=(sv, 1))
            sv_red = sv.n
            em("scalar", lambda e, xv=xv: e.activation(
                scrap, xv, AF.Square, scale=1.0, accum_out=ln_sb[:, 1:2]),
                waits=[(sd_x, 16 * (t + 1))], inc=(sa, 1))
            sa_sq = sa.n
            em("vector", lambda e: e.tensor_scalar(ln_sb[:, 2:3], ln_sb[:, 0:1], 1.0 / E, 0.0, OP.mult, OP.bypass),
               waits=[(sv, sv_red)], inc=(sv, 1))
            em("vector", lambda e: e.tensor_scalar(ln_sb[:, 3:4], ln_sb[:, 1:2], 1.0 / E, 0.0, OP.mult, OP.bypass),
               waits=[(sa, sa_sq)], inc=(sv, 1))
            em("vector", lambda e: e.tensor_tensor(ln_sb[:, 4:5], ln_sb[:, 2:3], ln_sb[:, 2:3], OP.mult),
               waits=[(sv, sv.n - 1)], inc=(sv, 1))
            em("vector", lambda e: e.tensor_tensor(ln_sb[:, 5:6], ln_sb[:, 3:4], ln_sb[:, 4:5], OP.subtract),
               waits=[(sv, sv.n)], inc=(sv, 1))
            em("scalar", lambda e: e.activation(ln_sb[:, 6:7], ln_sb[:, 5:6], AF.Sqrt,
                                                bias=ln_sb[:, 16:17], scale=1.0),
               waits=[(sv, sv.n)] + ([(sg, 3)] if t == 0 else []), inc=(sa, 1))
            em("vector", lambda e: e.reciprocal(ln_sb[:, 7:8], ln_sb[:, 6:7]), waits=[(sa, sa.n)], inc=(sv, 1))
            em("vector", lambda e: e.tensor_tensor(ln_sb[:, 8:9], ln_sb[:, 2:3], ln_sb[:, 7:8], OP.mult),
               waits=[(sv, sv.n)], inc=(sv, 1))
            em("vector", lambda e: e.tensor_scalar(ln_sb[:, 9:10], ln_sb[:, 8:9], -1.0, 0.0, OP.mult, OP.bypass),
               waits=[(sv, sv.n)], inc=(sv, 1))
            em("vector", lambda e, xv=xv, xnv=xnv: e.tensor_scalar(
                xnv, xv, ln_sb[:, 7:8], ln_sb[:, 9:10], OP.mult, OP.add),
               waits=[(sv, sv.n)], inc=(sv, 1))
            sv_ln = sv.n
            tw = [(sv, sv_ln)]
            if t >= 1:
                tw.append((sv, tev[t - 1]))
            if t == 0:
                tw.append((sd_s, 16))
            for ec in range(NEC):
                em("tensor", lambda e, ec=ec, xnv=xnv: e.transpose(
                    ps_c[:, ec * 128:ec * 128 + 128], xnv[:, ec * 128:ec * 128 + 128], id_sb),
                    waits=tw if ec == 0 else [], inc=(sp, 1) if ec == NEC - 1 else None)
            em("vector", lp(lambda e, xtb=xtb: e.tensor_copy(xtb, ps_c[:, 0:1024])), waits=[(sp, sp.n)],
               inc=(sv, 1))
            em("vector", lp(lambda e, xtr=xtr: e.tensor_copy(xtr, ps_c[:, 0:1024])), inc=(sv, 1))
            tev.append(sv.n)

        if NTC_:
            emit_loads(0)
            emit_loads(1)
            emit_lnT(0)
        for t in range(NTC_):
            if t + 1 < NTC_:
                emit_lnT(t + 1)
            xv, xnv, csv, snv = x_v(t), xn_v(t), cs_v(t), sn_v(t)
            xtb, xtr = xtb_v(t), xtr_v(t)
            sv_tevac = tev[t]
            for qi, (kind, half) in enumerate(
                    ((0, 0), (1, 0), (2, 0), (3, 0), (0, 1), (1, 1), (2, 1), (3, 1))):
                gseq = t * 8 + qi
                bank = ps_a[:, (gseq % 4) * 512:(gseq % 4) * 512 + 512]
                w0 = [(sv, sv_tevac)]
                if gseq >= 4:
                    w0.append((sv, qtr_ev[gseq - 4]))
                if gseq == 0:
                    w0 += [(sd_g, sd_g_wts), (sg, 4)]
                lhs = (lambda ec, xtb=xtb: xtb[:, ec * 128:ec * 128 + 128]) if kind < 3 else \
                      (lambda ec, xtr=xtr: xtr[:, ec * 128:ec * 128 + 128])
                rhs = {0: wq_t, 1: wk_t, 2: wv_t, 3: mm_t}[kind]
                for ec in range(NEC):
                    em("tensor", lambda e, lhs=lhs, rhs=rhs, ec=ec, h2=half, b=bank, s0=(ec == 0): e.matmul(
                        b, lhs(ec), rhs(ec, h2), start=s0, stop=False), waits=w0 if ec == 0 else [])
                if kind == 0:
                    em("tensor", lambda e, h2=half, b=bank: e.matmul(
                        b, onesb[0:1, 0:128], bias_a[0:1, h2 * 512:h2 * 512 + 512], start=False, stop=True),
                        inc=(sp, 1))
                elif kind == 1:
                    em("tensor", lambda e, h2=half, b=bank: e.matmul(
                        b, onesb[32:33, 0:128], bias_a[32:33, h2 * 512:h2 * 512 + 512], start=False, stop=True,
                        tile_position=(32, 0)), inc=(sp, 1))
                elif kind == 2:
                    em("tensor", lambda e, h2=half, b=bank: e.matmul(
                        b, onesb[64:65, 0:128], bias_a[64:65, h2 * 512:h2 * 512 + 512], start=False, stop=True,
                        tile_position=(64, 0)), inc=(sp, 1))
                else:
                    em("tensor", lambda e, h2=half, b=bank: e.matmul(
                        b, onesb[0:1, 0:128], onesb[0:1, 128 + h2 * 512:128 + h2 * 512 + 512],
                        start=False, stop=True), inc=(sp, 1))
                sp_q = sp.n
                if kind in (0, 1):
                    rot = (rq_v if kind == 0 else rk_v)(t)[:, half * 512:half * 512 + 512]
                    pv = bank.rearrange("p (h d) -> p h d", d=64)
                    uv = ru_v.rearrange("p (h d) -> p h d", d=64)
                    snh = snv[:, half * 512:half * 512 + 512].rearrange("p (h d) -> p h d", d=64)
                    vw0 = [(sp, sp_q), (sd_cs, 32 * (t + 1))]
                    if t >= 2:
                        vw0.append((sd_sp, 32 * (t - 1)))
                    em("vector", lp(lambda e, pv=pv, uv=uv, snh=snh: e.tensor_tensor(
                        uv[:, :, 0:32], pv[:, :, 32:64], snh[:, :, 0:32], OP.mult)), waits=vw0, inc=(sv, 1))
                    em("vector", lp(lambda e, pv=pv, uv=uv, snh=snh: e.tensor_tensor(
                        uv[:, :, 32:64], pv[:, :, 0:32], snh[:, :, 32:64], OP.mult)), inc=(sv, 1))
                    em("vector", lp(lambda e, b=bank, csv=csv, half=half: e.tensor_tensor(
                        rt_v, b, csv[:, half * 512:half * 512 + 512], OP.mult)), inc=(sv, 1))
                    em("vector", lp(lambda e, rot=rot: e.tensor_tensor(rot, ru_v, rt_v, OP.add)),
                       waits=[(sv, sv.n)], inc=(sv, 1))
                elif kind == 2:
                    vw0 = [(sp, sp_q)]
                    if gseq <= 8:
                        vw0.append((sg, 4))
                    if half == 0 and t >= 1:
                        vw0.append((sd_v, 16 * t))
                    dst = vst[:, half * 8 * VW: half * 8 * VW + 8 * VW].rearrange(
                        "p (h w) -> p h w", w=VW)[:, :, 0:64]
                    em("vector", lp(lambda e, b=bank, dst=dst: e.tensor_copy(
                        dst, b.rearrange("p (h d) -> p h d", d=64))), waits=vw0, inc=(sv, 1))
                else:
                    dst = pp_v(t)[:, half * 512:half * 512 + 512]
                    fw = [(sp, sp_q)]
                    if t >= 2:
                        fw.append((sd_pa, 16 * (t - 1)))
                    em("vector", lambda e, b=bank, dst=dst, xv=xv, half=half: e.tensor_tensor(
                        dst, b, xv[:, half * 512:half * 512 + 512], OP.add), waits=fw, inc=(sv, 1))
                qtr_ev.append(sv.n)
            if dbg:
                em("sync", lambda e, t=t: e.dma_start(out=xnd_d[t * 128:(t + 1) * 128, :], in_=xn_v(t)),
                   waits=[(sv, qtr_ev[t * 8 + 7])], inc=(sd_s, 16))
                em("sync", lambda e, t=t: e.dma_start(out=xtd_d[t, :, :], in_=xtb_v(t)),
                   waits=[(sv, qtr_ev[t * 8 + 7])], inc=(sd_s, 16))
            em("sync", lambda e, t=t: e.dma_start(out=qr_d[t * 128:(t + 1) * 128, :], in_=rq_v(t)),
               waits=[(sv, qtr_ev[t * 8 + 4])], inc=(sd_sp, 16))
            em("sync", lambda e, t=t: e.dma_start(out=kr_d[t * 128:(t + 1) * 128, :], in_=rk_v(t)),
               waits=[(sv, qtr_ev[t * 8 + 5])], inc=(sd_sp, 16))
            em("sync", lambda e, t=t: e.dma_start(out=pa_d[t * 128:(t + 1) * 128, :], in_=pp_v(t)),
               waits=[(sv, qtr_ev[t * 8 + 7])], inc=(sd_pa, 16))
            em("sync", lambda e, t=t: e.dma_start(out=v_d[t * 128:(t + 1) * 128, :], in_=vst),
               waits=[(sv, qtr_ev[t * 8 + 6])], inc=(sd_v, 16))
            tc_dve_done.append(sv.n)
            tc_sa_done.append(sa.n)
            if t + 2 < NTC:
                emit_loads(t + 2)
        sv_P_done = sv.n
        sp_P_done = sp.n

        def aot_sb(c):
            if c < 6:
                return ar2_bf[:, c * 4096:(c + 1) * 4096]
            return ar1_bf[:, (c - 6) * 4096:(c - 5) * 4096]

        qT = lambda ps0: ar1_bf[:, ps0 * 4096: ps0 * 4096 + 4096]
        kT = lambda ps0: ar1_bf[:, 16384 + ps0 * 4096: 16384 + ps0 * 4096 + 4096]

        exp_total = 0
        gqb = 0
        store_done_sv = []
        pair_sp_done = []

        def emit_vload(gp):
            w = [(sd_v, 16 * NTC), (sp, sp_B_done)] if gp < 2 else [(sp, pair_sp_done[gp - 2])]
            for s4 in range(4):
                em("sync", lambda e, c=gp, s4=s4, gp=gp: e.dma_start(
                    out=vbuf(gp)[:, s4 * 1040: s4 * 1040 + 1040].rearrange("p (c2 f) -> p c2 f", f=130),
                    in_=v_d[s4 * 1024:(s4 + 1) * 1024, c * 130:(c + 1) * 130].rearrange(
                        "(c2 p) f -> p c2 f", p=128)),
                    waits=w if s4 == 0 else [], inc=(sd_vl, 16))

        dlv = 0
        if "D" in phases:
            i_ = phases.index("D")
            dlv = int(phases[i_ + 1]) if i_ + 1 < len(phases) and phases[i_ + 1].isdigit() else 9
        DGRP = 2 if dlv >= 1 else 0
        do_exp = "X" not in phases
        if DGRP:
            emit_vload(0)
            emit_vload(1)
        for g in range(DGRP):
            if g == 0:
                w_tr = [(sd_sp, 32 * NTC), (sd_pa, 16 * NTC), (sv, sv_P_done)]
            else:
                w_tr = [(sp, sp.n)]
            for ps0 in range(4):
                em("sync", lambda e, g=g, ps0=ps0: e.dma_start(
                    out=qT(ps0), in_=qr_d[:, g * 512 + ps0 * 128: g * 512 + ps0 * 128 + 128],
                    transpose=True), waits=w_tr if ps0 == 0 else [], inc=(sd_tr, 16))
                em("sync", lambda e, g=g, ps0=ps0: e.dma_start(
                    out=kT(ps0), in_=kr_d[:, g * 512 + ps0 * 128: g * 512 + ps0 * 128 + 128],
                    transpose=True), inc=(sd_tr, 16))
            sd_tr_g = sd_tr.n
            for ps0 in range(4):
                chunk = g * 4 + ps0
                for qb in range(NQB):
                    q0 = qb * QB
                    sa_base = sa.n
                    sc_sp = []
                    abank = ps_b if gqb % 2 == 0 else ps_c

                    def emit_av(kc_av, abank=abank):
                        avw = [(sa, sa_base + kc_av + 1)]
                        if kc_av == 0 and gqb >= 2:
                            avw.append((sv, store_done_sv[gqb - 2]))
                        if kc_av == 0 and qb == 0:
                            avw.append((sd_vl, 64 * (chunk + 1)))
                        for hj in range(2):
                            em("tensor", lambda e, kc=kc_av, hj=hj, gp=chunk, abank=abank: e.matmul(
                                abank[0:65, hj * 512:hj * 512 + 512],
                                vbuf(gp)[:, kc * 130 + hj * VW: kc * 130 + hj * VW + VW],
                                ex_bf[0:128, (kc % 3) * 1024 + hj * 512:(kc % 3) * 1024 + hj * 512 + 512],
                                start=(kc == 0), stop=(kc == NKC - 1)),
                                waits=avw if hj == 0 else [], inc=(sp, 1) if hj == 1 else None)

                    for kc in range(NKC if dlv >= 2 else 0):
                        scw = []
                        if kc == 0 and qb == 0:
                            scw.append((sd_tr, sd_tr_g))
                        if do_exp and kc >= 2:
                            scw.append((sa, sa_base + kc - 1))
                        elif do_exp and gqb > 0:
                            scw.append((sa, sa_base))
                        sbank = ps_a[:, (kc % 2) * 1024:(kc % 2) * 1024 + 1024]
                        em("tensor", lambda e, ps0=ps0, kc=kc, q0=q0, b=sbank: e.matmul(
                            b[:, 0:512], kT(ps0)[0:64, kc * 128:kc * 128 + 128],
                            qT(ps0)[0:64, q0:q0 + QB], start=True, stop=True), waits=scw)
                        em("tensor", lambda e, ps0=ps0, kc=kc, q0=q0, b=sbank: e.matmul(
                            b[:, 512:1024], kT(ps0)[64:128, kc * 128:kc * 128 + 128],
                            qT(ps0)[64:128, q0:q0 + QB], start=True, stop=True,
                            tile_position=(64, 0)), inc=(sp, 1))
                        sc_sp.append(sp.n)
                        ew = [(sp, sc_sp[kc])]
                        if exp_total == 0:
                            ew.append((sg, 3))
                            ew.append((sd_v, 16 * NTC))
                        if do_exp:
                            em("scalar", lp(lambda e, kc=kc, b=sbank: e.activation(
                                ex_bf[:, (kc % 3) * 1024:(kc % 3) * 1024 + 1024], b, AF.Exp,
                                bias=ln_sb[:, 24:25], scale=0.125)), waits=ew, inc=(sa, 1))
                        exp_total += 1
                        if kc >= 1 and dlv >= 3:
                            emit_av(kc - 1)
                    if dlv >= 3:
                        emit_av(NKC - 1)
                    sp_av_last = sp.n
                    if dlv >= 4:
                        cw = [(sp, sp_av_last)]
                        if gqb >= 2:
                            cw.append((sd_ao, 16 * (gqb - 1)))
                        for hj in range(2):
                            em("vector", lp(lambda e, hj=hj, abank=abank, gqb=gqb: e.tensor_copy(
                                as_v(gqb)[hj * 64:hj * 64 + 64, :],
                                abank[0:64, hj * 512:hj * 512 + 512])),
                                waits=cw if hj == 0 else [], inc=(sv, 1))
                        rw = [(sd_rs, 16 * (gqb - 1))] if gqb >= 2 else []
                        em("vector", lp(lambda e, abank=abank, gqb=gqb: e.tensor_copy(
                            rs_v(gqb)[64:65, 0:1024], abank[64:65, 0:1024])), waits=rw, inc=(sv, 1))
                        store_done_sv.append(sv.n)
                        em("sync", lambda e, chunk=chunk, qb=qb, gqb=gqb: e.dma_start(
                            out=ao_d[chunk, :, qb * QB:(qb + 1) * QB], in_=as_v(gqb)),
                            waits=[(sv, sv.n)], inc=(sd_ao, 16))
                        em("sync", lambda e, gqb=gqb: e.dma_start(
                            out=rs_d[gqb:gqb + 1, :], in_=rs_v(gqb)[64:65, 0:1024]),
                            waits=[(sv, sv.n)], inc=(sd_rs, 16))
                    else:
                        store_done_sv.append(sv.n)
                    gqb += 1
                pair_sp_done.append(sp.n)
                if chunk + 2 < 8:
                    emit_vload(chunk + 2)
                if chunk < 6 and "E" in phases and "Y" not in phases:
                    w = [(sd_ao, 16 * NQB * (chunk + 1))]
                    if chunk == 0:
                        w.append((sp, sp_P_done))
                    em("gpsimd", lambda e, c=chunk: e.dma_start(out=aot_sb(c), in_=ao_d[c, :, :]),
                       waits=w, inc=(sd_pe, 16))
        sp_D_done = sp.n

        aot_list = ((list(range(8)) if "Y" in phases else [6, 7]) if "E" in phases else [])
        for c in aot_list:
            w = [(sd_ao, 16 * NQB * (c + 1)), (sp, sp_D_done)]
            em("sync", lambda e, c=c: e.dma_start(out=aot_sb(c), in_=ao_d[c, :, :]), waits=w, inc=(sd_pe, 16))

        rssb = ar1_bf[:, 16384:17408]
        rsrec = ar1_bf[:, 17408:18432]
        rsflat = lambda i: ar1_bf[0:1, 18432:26624]
        presc_sv = {}
        if "E" in phases and "Z" not in phases:
            em("sync", lambda e: e.dma_start(out=rssb[0:64, :], in_=rs_d),
               waits=[(sd_rs, 16 * 64), (sp, sp_D_done)], inc=(sd_rr, 16))
            em("vector", lp(lambda e: e.reciprocal(rsrec[0:64, :], rssb[0:64, :])),
               waits=[(sd_rr, 16)], inc=(sv, 1))
            em("sync", lambda e: e.dma_start(out=rsr_d, in_=rsrec[0:64, :]),
               waits=[(sv, sv.n)], inc=(sd_rr, 16))
            for c in range(8):
                fw = [(sd_rr, 32)] if c == 0 else [(sv, presc_sv[(c - 1, NQB - 1)])]
                em("sync", lambda e, c=c: e.dma_start(
                    out=rsflat(c).rearrange("o (a f) -> o a f", a=8),
                    in_=rsr_d[c * 8:(c + 1) * 8, :].unsqueeze(0)),
                    waits=fw, inc=(sd_rr, 16))
                for qb in range(NQB):
                    i = c * NQB + qb
                    dslot = ps_c[:, (i % 2) * 512:(i % 2) * 512 + 512]
                    bw = [(sd_rr, sd_rr.n)]
                    if i >= 2:
                        bw.append((sv, presc_sv[(c, qb - 2) if qb >= 2 else (c - 1, NQB - 2 + qb)]))
                    em("tensor", lambda e, c=c, qb=qb, dslot=dslot: e.matmul(
                        dslot[0:64, :], onesb[0:1, 0:64], rsflat(c)[0:1, qb * 1024: qb * 1024 + 512],
                        start=True, stop=True), waits=bw)
                    em("tensor", lambda e, c=c, qb=qb, dslot=dslot: e.matmul(
                        dslot[64:128, :], onesb[0:1, 0:64],
                        rsflat(c)[0:1, qb * 1024 + 512: qb * 1024 + 1024],
                        start=True, stop=True, tile_position=(0, 64)), inc=(sp, 1))
                    em("vector", lp(lambda e, c=c, qb=qb, dslot=dslot: e.tensor_tensor(
                        aot_sb(c)[:, qb * QB:(qb + 1) * QB],
                        aot_sb(c)[:, qb * QB:(qb + 1) * QB], dslot, OP.mult)),
                        waits=[(sp, sp.n), (sd_pe, 16 * (c + 1))], inc=(sv, 1))
                    presc_sv[(c, qb)] = sv.n

        ev_e = []

        def emit_pa(t):
            pw = [(sv, ev_e[t - 4])] if t >= 4 else [(sd_pa, 16 * NTC), (sp, sp_D_done)]
            em("sync", lambda e, t=t: e.dma_start(
                out=pa_e(t), in_=pa_d[t * 128:(t + 1) * 128, :]), waits=pw, inc=(sd_pe, 16))

        if "E" in phases:
            for t in range(4):
                emit_pa(t)
        for t in range(NTC if "E" in phases else 0):
            bank = ps_a[:, (t % 2) * 1024:(t % 2) * 1024 + 1024]
            w0 = [(sd_pe, 16 * 8 + 16 * (t + 1))]
            if t >= 2:
                w0.append((sv, ev_e[t - 2]))
            if "Z" not in phases:
                w0.append((sv, presc_sv[(7, t // 4)]))
            for c in range(8):
                em("tensor", lambda e, c=c, t=t, b=bank: e.matmul(
                    b[:, 0:512], aot_sb(c)[:, t * 128:(t + 1) * 128], wo_t(c, 0),
                    start=(c == 0), stop=False), waits=w0 if c == 0 else [])
                em("tensor", lambda e, c=c, t=t, b=bank: e.matmul(
                    b[:, 512:1024], aot_sb(c)[:, t * 128:(t + 1) * 128], wo_t(c, 1),
                    start=(c == 0), stop=(c == 7)), inc=(sp, 1) if c == 7 else None)
            ttw = [(sp, sp.n)]
            if t >= 2:
                ttw.append((sd_out, 16 * (t - 1)))
            em("vector", lambda e, t=t, b=bank: e.tensor_tensor(ou_e(t), b, pa_e(t), OP.add),
               waits=ttw, inc=(sv, 1))
            ev_e.append(sv.n)
            em("sync", lambda e, t=t: e.dma_start(out=out_d[t * 128:(t + 1) * 128, :], in_=ou_e(t)),
               waits=[(sv, sv.n)], inc=(sd_out, 16))
            if t + 4 < NTC:
                emit_pa(t + 4)
        if "E" in phases:
            em("sync", lambda e: None, waits=[(sd_out, sd_out.n)])

        nc._prog_debug = (prog, sems)
        block = stk.enter_context(nc.Block())
        sections = {"sync": block.sync, "vector": block.vector, "scalar": block.scalar,
                    "tensor": block.tensor, "gpsimd": block.gpsimd}

        def emitter(name):
            def body(eng):
                for (waits, fn, inc) in prog[name]:
                    for (sem, val) in waits:
                        if val > 0:
                            eng.wait_ge(sem.h, val)
                    inst = fn(eng)
                    if inc is not None and inst is not None:
                        inst.then_inc(inc[0].h, inc[1])
            return body

        for name, section in sections.items():
            section(emitter(name))
    return nc


def kernel(L0_attn_x, L0_attn_wq, L0_attn_bq, L0_attn_wk, L0_attn_bk,
           L0_attn_wv, L0_attn_bv, L0_attn_wo, L0_attn_bo,
           L0_ffn_w_freq_real, L0_ffn_w_freq_imag, L0_ffn_bias):
    f32 = np.float32
    bf16 = ml_dtypes.bfloat16
    x = np.asarray(L0_attn_x, f32)
    wq = np.asarray(L0_attn_wq, f32); bq = np.asarray(L0_attn_bq, f32)
    wk = np.asarray(L0_attn_wk, f32); bk = np.asarray(L0_attn_bk, f32)
    wv = np.asarray(L0_attn_wv, f32); bv = np.asarray(L0_attn_bv, f32)
    wo = np.asarray(L0_attn_wo, f32); bo = np.asarray(L0_attn_bo, f32)
    wr = np.asarray(L0_ffn_w_freq_real, f32)
    wi = np.asarray(L0_ffn_w_freq_imag, f32)
    bffn = np.asarray(L0_ffn_bias, f32)

    idx = (np.outer(np.arange(E, dtype=np.int64), np.arange(E, dtype=np.int64)) % E)
    ang = (2.0 * np.pi / E) * idx.astype(np.float64)
    cmat = np.cos(ang)
    smat = np.sin(ang)
    ch = cmat.astype(bf16)
    cl = (cmat - ch.astype(np.float64)).astype(bf16)
    sh = smat.astype(bf16)
    sl = (smat - sh.astype(np.float64)).astype(bf16)
    inv_freq = 1.0 / 10000.0 ** (np.arange(0, HD, 2, dtype=np.float64) / HD)
    freqs = np.outer(np.arange(S, dtype=np.float64), inv_freq)
    cos64 = np.repeat(np.cos(freqs), 2, axis=1)
    sin64 = np.repeat(np.sin(freqs), 2, axis=1)
    sin64[:, :32] *= -1.0
    cost = np.ascontiguousarray(np.tile(cos64, (1, H)).astype(f32))
    sint = np.ascontiguousarray(np.tile(sin64, (1, H)).astype(f32))
    ident = np.eye(128, dtype=f32)

    common = {
        "wqT": np.ascontiguousarray(wq.T), "wkT": np.ascontiguousarray(wk.T),
        "wvT": np.ascontiguousarray(wv.T), "woT": np.ascontiguousarray(wo.T),
        "bq": bq.reshape(1, E), "bk": bk.reshape(1, E), "bv": bv.reshape(1, E),
        "bo": bo.reshape(1, E), "bffn": bffn.reshape(1, E),
        "wfr": np.ascontiguousarray(wr), "wfi": np.ascontiguousarray(wi),
        "ch": ch, "cl": cl, "sh": sh, "sl": sl,
        "cost": cost, "sint": sint, "ident": ident,
    }
    in_maps = []
    for c in range(NCORES):
        m = dict(common)
        m["x"] = np.ascontiguousarray(x[c])
        in_maps.append(m)

    nc = build_nc()
    res = run_bass_kernel_spmd(nc, in_maps, core_ids=list(range(NCORES)))
    return np.stack([np.asarray(r["out"], f32) for r in res.results], axis=0)


# version 16 (final)
# speedup vs baseline: 1.0292x; 1.0292x over previous
import numpy as np
import ml_dtypes
from contextlib import ExitStack
import concourse.bass as bass
import concourse.mybir as mybir
from concourse.bass_utils import run_bass_kernel_spmd
from concourse.alu_op_type import AluOpType as OP

F32 = mybir.dt.float32
F32R = mybir.dt.float32r
BF16 = mybir.dt.bfloat16
AF = mybir.ActivationFunctionType
AX = mybir.AxisListType

S, E, H, HD = 4096, 1024, 16, 64
NTC = S // 128
NEC = E // 128
SHIFT = 12.0
EPS = 1e-5
NCORES = 8
VW = 65
QB = 512
NQB = S // QB
NKC = NTC


class Sem:
    def __init__(self, h):
        self.h = h
        self.n = 0

    def inc(self, k=1):
        self.n += k
        return self.n


def build_nc(phases="BGPDE", dbg=False):
    nc = bass.Bass("TRN2", target_bir_lowering=False, debug=False)
    DT = nc.dram_tensor
    x_d = DT("x", [S, E], F32, kind="ExternalInput").ap()
    wq_d = DT("wqT", [E, E], F32, kind="ExternalInput").ap()
    wk_d = DT("wkT", [E, E], F32, kind="ExternalInput").ap()
    wv_d = DT("wvT", [E, E], F32, kind="ExternalInput").ap()
    wo_d = DT("woT", [E, E], F32, kind="ExternalInput").ap()
    bq_d = DT("bq", [1, E], F32, kind="ExternalInput").ap()
    bk_d = DT("bk", [1, E], F32, kind="ExternalInput").ap()
    bv_d = DT("bv", [1, E], F32, kind="ExternalInput").ap()
    bo_d = DT("bo", [1, E], F32, kind="ExternalInput").ap()
    bf_d = DT("bffn", [1, E], F32, kind="ExternalInput").ap()
    wr_d = DT("wfr", [E, E], F32, kind="ExternalInput").ap()
    wi_d = DT("wfi", [E, E], F32, kind="ExternalInput").ap()
    ch_d = DT("ch", [E, E], BF16, kind="ExternalInput").ap()
    cl_d = DT("cl", [E, E], BF16, kind="ExternalInput").ap()
    sh_d = DT("sh", [E, E], BF16, kind="ExternalInput").ap()
    sl_d = DT("sl", [E, E], BF16, kind="ExternalInput").ap()
    ct_d = DT("cost", [S, E], F32, kind="ExternalInput").ap()
    st_d = DT("sint", [S, E], F32, kind="ExternalInput").ap()
    id_d = DT("ident", [128, 128], F32, kind="ExternalInput").ap()
    out_d = DT("out", [S, E], F32, kind="ExternalOutput").ap()
    KI = "ExternalOutput" if dbg else "Internal"
    qr_d = DT("qrot", [S, E], BF16, kind=KI).ap()
    kr_d = DT("krot", [S, E], BF16, kind=KI).ap()
    pa_d = DT("partial", [S, E], F32, kind=KI).ap()
    ao_d = DT("attnOT", [8, 128, S], BF16, kind=KI).ap()
    v_d = DT("vaug", [S, 16 * VW], BF16, kind=KI).ap()
    rs_d = DT("rowsums", [64, 1024], BF16, kind="Internal").ap()
    rsr_d = DT("rowrecip", [64, 1024], BF16, kind="Internal").ap()
    md_d = DT("madump", [E, E], F32, kind="ExternalOutput").ap() if dbg else None
    xnd_d = DT("xndump", [S, E], F32, kind="ExternalOutput").ap() if dbg else None
    xtd_d = DT("xtdump", [NTC, 128, E], BF16, kind="ExternalOutput").ap() if dbg else None

    with ExitStack() as stk:
        EN = stk.enter_context
        ar1 = EN(nc.sbuf_tensor("ar1", [128, 16384], F32))
        ar2 = EN(nc.sbuf_tensor("ar2", [128, 16384], F32))
        m_r = EN(nc.sbuf_tensor("m_r", [128, 8192], F32R))
        xnt_r = EN(nc.sbuf_tensor("xnt_r", [128, 2048], F32R))
        vsl = EN(nc.sbuf_tensor("vsl", [128, 8320], BF16))
        ar4 = EN(nc.sbuf_tensor("ar4", [128, 3072], F32))
        bias_a = EN(nc.sbuf_tensor("bias_a", [65, 1024], BF16))
        onesb = EN(nc.sbuf_tensor("onesb", [65, 1152], BF16))
        ln_sb = EN(nc.sbuf_tensor("ln_sb", [128, 32], F32))
        ps_a = EN(nc.psum_tensor("ps_a", [128, 2048], F32))
        ps_b = EN(nc.psum_tensor("ps_b", [128, 1024], F32))
        ps_c = EN(nc.psum_tensor("ps_c", [128, 1024], F32))

        sems = {}
        for name in ("sp", "sv", "sa", "sg", "sd_x", "sd_cs", "sd_s", "sd_g",
                     "sd_sp", "sd_pa", "sd_tr", "sd_ao", "sd_pe", "sd_out",
                     "sd_v", "sd_vl", "sd_rs", "sd_rr"):
            sems[name] = Sem(EN(nc.semaphore(name)))
        sp, sv, sa, sg = sems["sp"], sems["sv"], sems["sa"], sems["sg"]
        sd_x, sd_cs, sd_s, sd_g = sems["sd_x"], sems["sd_cs"], sems["sd_s"], sems["sd_g"]
        sd_sp, sd_pa, sd_tr, sd_ao = sems["sd_sp"], sems["sd_pa"], sems["sd_tr"], sems["sd_ao"]
        sd_pe, sd_out = sems["sd_pe"], sems["sd_out"]
        sd_v, sd_vl = sems["sd_v"], sems["sd_vl"]
        sd_rs, sd_rr = sems["sd_rs"], sems["sd_rr"]

        prog = {k: [] for k in ("tensor", "vector", "scalar", "sync", "gpsimd")}

        def em(eng, fn, waits=(), inc=None):
            if inc is not None:
                inc[0].inc(inc[1])
            prog[eng].append((tuple(waits), fn, inc))

        def lp(f):
            def g(e):
                with nc.allow_low_precision(reason="intentional bf16/f32r rounding"):
                    return f(e)
            return g

        ar1_bf = ar1[:].bitcast(BF16)
        ar2_bf = ar2[:].bitcast(BF16)

        x_v = lambda t: ar1[:, (t % 2) * E: (t % 2) * E + E]
        xn_v = lambda t: ar1[:, 2048 + (t % 2) * E: 2048 + (t % 2) * E + E]
        cs_v = lambda t: ar1[:, 4096 + (t % 2) * E: 4096 + (t % 2) * E + E]
        sn_v = lambda t: ar1[:, 6144 + (t % 2) * E: 6144 + (t % 2) * E + E]
        pp_v = lambda t: ar1[:, 8192 + (t % 2) * E: 8192 + (t % 2) * E + E]
        xtb_v = lambda t: ar1_bf[:, 20480 + (t % 2) * E: 20480 + (t % 2) * E + E]
        rq_v = lambda t: ar1_bf[:, 22528 + (t % 2) * E: 22528 + (t % 2) * E + E]
        rk_v = lambda t: ar1_bf[:, 24576 + (t % 2) * E: 24576 + (t % 2) * E + E]
        ru_v = ar1_bf[:, 26624:27136]
        rt_v = ar1_bf[:, 27136:27648]
        xtr_v = lambda t: xnt_r[:, (t % 2) * E: (t % 2) * E + E]
        pa_e = lambda t: ar1[:, 4096 + (t % 4) * 1024: 4096 + (t % 4) * 1024 + 1024]
        ou_e = lambda t: ar1[:, 13312 + (t % 2) * 1024: 13312 + (t % 2) * 1024 + 1024]

        stgin = ar4[:, 0:1024]
        stg2 = ar4[:, 1536:2560]
        stg_i = lambda i: (stgin, stg2)[i % 2]
        scrap = ar4[:, 0:1024]
        tmp1 = ar4[:, 1024:2048]
        tA = ar4[:, 1024:1152]
        tB = ar4[:, 1152:1280]
        tC = ar4[:, 1280:1408]
        m_part = ar4[:, 1536:2560]
        id_sb = ar4[:, 2560:2688]
        ex_bf = ar4[:, 0:1536].bitcast(BF16)
        as_v = lambda i: ar4[:, 1536 + (i % 2) * 256: 1536 + (i % 2) * 256 + 256].bitcast(BF16)
        rs_v = lambda i: ar4[:, 2048 + (i % 2) * 512: 2048 + (i % 2) * 512 + 512].bitcast(BF16)
        vst = ar4[:, 1024:1544].bitcast(BF16)

        chf = lambda jc, oh: ar1_bf[:, jc * 1024 + oh * 512: jc * 1024 + oh * 512 + 512]
        clf = lambda jc, oh: ar1_bf[:, 8192 + jc * 1024 + oh * 512: 8192 + jc * 1024 + oh * 512 + 512]
        shf = lambda jc, oh: ar1_bf[:, 16384 + jc * 1024 + oh * 512: 16384 + jc * 1024 + oh * 512 + 512]
        slf = lambda jc, oh: ar1_bf[:, 24576 + jc * 1024 + oh * 512: 24576 + jc * 1024 + oh * 512 + 512]
        wrh = lambda kc, ac: ar2_bf[:, kc * 1024 + ac * 128: kc * 1024 + ac * 128 + 128]
        wrl = lambda kc, ac: ar2_bf[:, 8192 + kc * 1024 + ac * 128: 8192 + kc * 1024 + ac * 128 + 128]
        wih = lambda kc, ac: ar2_bf[:, 16384 + kc * 1024 + ac * 128: 16384 + kc * 1024 + ac * 128 + 128]
        wil = lambda kc, ac: ar2_bf[:, 24576 + kc * 1024 + ac * 128: 24576 + kc * 1024 + ac * 128 + 128]
        t1h = vsl[:, 0:1024]
        t1l = vsl[:, 1024:2048]
        t2h = vsl[:, 2048:3072]
        t2l = vsl[:, 3072:4096]
        rsl = lambda base, kc, rc: ar1_bf[:, base + kc * 1024 + rc * 128: base + kc * 1024 + rc * 128 + 128]
        vbuf = lambda i: vsl[:, (i % 2) * 4160: (i % 2) * 4160 + 4160]

        em("gpsimd", lambda e: e.memset(onesb[:], 1.0), inc=(sg, 1))
        em("gpsimd", lambda e: e.memset(ln_sb[:, 16:17], EPS), inc=(sg, 1))
        em("gpsimd", lambda e: e.memset(ln_sb[:, 24:25], -SHIFT), inc=(sg, 1))
        em("sync", lambda e: e.dma_start(out=id_sb, in_=id_d), inc=(sd_s, 16))
        em("gpsimd", lambda e: e.dma_start(out=bias_a[0:1, :], in_=bq_d), inc=(sd_g, 16))
        em("gpsimd", lambda e: e.dma_start(out=bias_a[32:33, :], in_=bk_d), inc=(sd_g, 16))
        em("gpsimd", lambda e: e.dma_start(out=bias_a[64:65, :], in_=bv_d), inc=(sd_g, 16))
        em("sync", lambda e: e.dma_start(out=tmp1[0:1, :], in_=bo_d), inc=(sd_s, 16))
        em("sync", lambda e: e.dma_start(out=scrap[0:1, :], in_=bf_d), inc=(sd_s, 16))
        em("vector", lp(lambda e: e.tensor_tensor(onesb[0:1, 128:1152], tmp1[0:1, :], scrap[0:1, :], OP.add)),
           waits=[(sd_s, sd_s.n), (sg, 1)], inc=(sv, 1))
        sv_bias = sv.n

        for src, off in ((ch_d, 0), (cl_d, 8192), (sh_d, 16384), (sl_d, 24576)):
            em("sync", lambda e, src=src, off=off: e.dma_start(
                out=ar1_bf[:, off:off + 8192].rearrange("p (c f) -> p c f", f=1024),
                in_=src.rearrange("(c p) f -> p c f", p=128)), inc=(sd_s, 16))
        sd_s_b = sd_s.n
        wsplit_sv = [sv_bias]
        wi_ctr = 0
        for i, (wsrc, offh, offl) in enumerate(((wr_d, 0, 8192), (wi_d, 16384, 24576))):
            for c in range(NEC):
                j = wi_ctr
                wi_ctr += 1
                stg = stg_i(j)
                w = [(sv, sv_bias)] if j < 2 else [(sv, wsplit_sv[j - 1])]
                em("gpsimd", lambda e, wsrc=wsrc, c=c, stg=stg: e.dma_start(
                    out=stg, in_=wsrc[c * 128:(c + 1) * 128, :]), waits=w, inc=(sd_g, 16))
                hi = ar2_bf[:, offh + c * 1024: offh + c * 1024 + 1024]
                lo = ar2_bf[:, offl + c * 1024: offl + c * 1024 + 1024]
                em("vector", lp(lambda e, hi=hi, stg=stg: e.tensor_copy(hi, stg)),
                   waits=[(sd_g, sd_g.n)], inc=(sv, 1))
                wtmp = vsl[:].bitcast(F32)[:, 0:1024]
                em("vector", lambda e, hi=hi, wtmp=wtmp: e.tensor_copy(wtmp, hi),
                   waits=[(sv, sv.n)], inc=(sv, 1))
                em("vector", lp(lambda e, lo=lo, stg=stg, wtmp=wtmp: e.tensor_tensor(
                    lo, stg, wtmp, OP.subtract)),
                   waits=[(sv, sv.n)], inc=(sv, 1))
                wsplit_sv.append(sv.n)
        sv_wsplit = sv.n

        CH, CL, SH, SL = 0, 8192, 16384, 24576
        st1a_pairs = ((wrh, CH), (wrl, CH), (wrh, CL), (wih, SH), (wil, SH), (wih, SL))
        g1 = ((wih, CH), (wil, CH), (wih, CL))
        g2 = ((wrh, SH), (wrl, SH), (wrh, SL))
        st2 = ((t1h, chf), (t1l, chf), (t1h, clf), (t2h, shf), (t2l, shf), (t2h, slf))
        t1ev_prev = t2ev_prev = None
        mpart_ev = {}
        dump_dma = 0
        for rc in range(NEC):
            t1ev = []
            for ac in range(NEC):
                bank = ps_a[:, (ac % 2) * 512:(ac % 2) * 512 + 128]
                if ac == 0 and rc == 0:
                    w0 = [(sv, sv_wsplit), (sd_s, sd_s_b)]
                elif ac < 2:
                    w0 = [(sv, t1ev_prev[6 + ac])] if rc else []
                else:
                    w0 = [(sv, t1ev[ac - 2])]
                first = True
                k = 0
                for kc in range(NEC):
                    for (lt, rbase) in st1a_pairs:
                        k += 1
                        last = k == len(st1a_pairs) * NEC
                        em("tensor", lambda e, lt=lt, rb=rbase, kc=kc, ac=ac, rc=rc, b=bank, s0=first, s1=last:
                           e.matmul(b, lt(kc, ac), rsl(rb, kc, rc), start=s0, stop=s1),
                           waits=w0 if first else [], inc=(sp, 1) if last else None)
                        first = False
                        w0 = []
                hs = t1h[:, ac * 128:ac * 128 + 128]
                ls = t1l[:, ac * 128:ac * 128 + 128]
                em("vector", lambda e, b=bank: e.tensor_scalar(tA, b, 1.0 / E, 0.0, OP.mult, OP.bypass),
                   waits=[(sp, sp.n)], inc=(sv, 1))
                em("vector", lp(lambda e, hs=hs: e.tensor_copy(hs, tA)), waits=[(sv, sv.n)], inc=(sv, 1))
                em("vector", lambda e, hs=hs: e.tensor_copy(tB, hs), waits=[(sv, sv.n)], inc=(sv, 1))
                em("vector", lp(lambda e, ls=ls: e.tensor_tensor(ls, tA, tB, OP.subtract)),
                   waits=[(sv, sv.n)], inc=(sv, 1))
                t1ev.append(sv.n)
            t2ev = []
            for ac in range(NEC):
                bA = ps_a[:, 1024 + (ac % 2) * 512: 1024 + (ac % 2) * 512 + 128]
                bB = ps_c[:, (ac % 2) * 512:(ac % 2) * 512 + 128]
                if ac < 2:
                    w0 = [(sv, t2ev_prev[6 + ac])] if rc else []
                else:
                    w0 = [(sv, t2ev[ac - 2])]
                for gi, (bk_, pairs) in enumerate(((bA, g1), (bB, g2))):
                    first = True
                    k = 0
                    for kc in range(NEC):
                        for (lt, rbase) in pairs:
                            k += 1
                            last = k == len(pairs) * NEC
                            em("tensor", lambda e, lt=lt, rb=rbase, kc=kc, ac=ac, rc=rc, b=bk_, s0=first, s1=last:
                               e.matmul(b, lt(kc, ac), rsl(rb, kc, rc), start=s0, stop=s1),
                               waits=w0 if (first and gi == 0) else [],
                               inc=(sp, 1) if (last and gi == 1) else None)
                            first = False
                hs = t2h[:, ac * 128:ac * 128 + 128]
                ls = t2l[:, ac * 128:ac * 128 + 128]
                em("vector", lambda e, b=bA: e.tensor_scalar(tA, b, 1.0 / E, 0.0, OP.mult, OP.bypass),
                   waits=[(sp, sp.n)], inc=(sv, 1))
                em("vector", lambda e, b=bB: e.tensor_scalar(tB, b, 1.0 / E, 0.0, OP.mult, OP.bypass),
                   inc=(sv, 1))
                em("vector", lambda e: e.tensor_tensor(tC, tB, tA, OP.subtract), waits=[(sv, sv.n)], inc=(sv, 1))
                em("vector", lp(lambda e, hs=hs: e.tensor_copy(hs, tC)), waits=[(sv, sv.n)], inc=(sv, 1))
                em("vector", lambda e, hs=hs: e.tensor_copy(tA, hs), waits=[(sv, sv.n)], inc=(sv, 1))
                em("vector", lp(lambda e, ls=ls: e.tensor_tensor(ls, tC, tA, OP.subtract)),
                   waits=[(sv, sv.n)], inc=(sv, 1))
                t2ev.append(sv.n)
            sv_stage1 = sv.n
            for oh in range(2):
                bank = ps_b[:, oh * 512: oh * 512 + 512]
                w0 = [(sv, sv_stage1)] if oh == 0 else []
                if rc:
                    w0 = w0 + [(sv, mpart_ev[(rc - 1, oh)])]
                first = True
                k = 0
                for jc in range(NEC):
                    for (lt, rt) in st2:
                        k += 1
                        last = k == len(st2) * NEC
                        em("tensor", lambda e, lt=lt, rt=rt, jc=jc, oh=oh, b=bank, s0=first, s1=last: e.matmul(
                            b, lt[:, jc * 128:jc * 128 + 128], rt(jc, oh), start=s0, stop=s1),
                            waits=w0 if first else [], inc=(sp, 1) if last else None)
                        first = False
                        w0 = []
                mw = [(sp, sp.n)]
                if dbg and rc:
                    mw.append((sd_s, dump_dma))
                em("vector", lp(lambda e, rc=rc, oh=oh, b=bank: e.tensor_copy(
                    m_r[:, rc * 1024 + oh * 512: rc * 1024 + oh * 512 + 512], b)),
                    waits=mw, inc=(sv, 1))
                if dbg:
                    em("vector", lambda e, oh=oh, b=bank: e.tensor_copy(
                        m_part[:, oh * 512:oh * 512 + 512], b), inc=(sv, 1))
                mpart_ev[(rc, oh)] = sv.n
            if dbg:
                em("sync", lambda e, rc=rc: e.dma_start(out=md_d[rc * 128:(rc + 1) * 128, :], in_=m_part),
                   waits=[(sv, mpart_ev[(rc, 1)])], inc=(sd_s, 16))
                dump_dma = sd_s.n
            t1ev_prev, t2ev_prev = t1ev, t2ev
        sp_B_done = sp.n
        sv_B_done = sv.n

        vst_ones = vst.rearrange("p (n w) -> p n w", w=VW)[:, :, 64:65]
        em("gpsimd", lambda e: e.memset(vst_ones, 1.0), waits=[(sv, sv_B_done), (sp, sp_B_done)], inc=(sg, 1))

        for wdd, off in ((wq_d, 0), (wk_d, 8192), (wv_d, 16384), (wo_d, 24576)):
            em("gpsimd", lambda e, wdd=wdd, off=off: e.dma_start(
                out=ar2_bf[:, off:off + 8192].rearrange("p (c f) -> p c f", f=1024),
                in_=wdd.rearrange("(c p) f -> p c f", p=128)),
                waits=[(sp, sp_B_done)] if off == 0 else [], inc=(sd_g, 16))
        sd_g_wts = sd_g.n
        wq_t = lambda ec, h2: ar2_bf[:, ec * 1024 + h2 * 512: ec * 1024 + h2 * 512 + 512]
        wk_t = lambda ec, h2: ar2_bf[:, 8192 + ec * 1024 + h2 * 512: 8192 + ec * 1024 + h2 * 512 + 512]
        wv_t = lambda ec, h2: ar2_bf[:, 16384 + ec * 1024 + h2 * 512: 16384 + ec * 1024 + h2 * 512 + 512]
        wo_t = lambda c, h2: ar2_bf[:, 24576 + c * 1024 + h2 * 512: 24576 + c * 1024 + h2 * 512 + 512]
        mm_t = lambda ec, h2: m_r[:, ec * 1024 + h2 * 512: ec * 1024 + h2 * 512 + 512]

        if "P" not in phases:
            NTC_ = 0
        else:
            NTC_ = NTC
        tc_dve_done = []
        tc_sa_done = []
        tev = []
        qtr_ev = []

        def emit_loads(tc):
            w = [(sv, tc_dve_done[tc - 2]), (sa, tc_sa_done[tc - 2])] if tc >= 2 else \
                [(sp, sp_B_done), (sv, sv_B_done)]
            em("sync", lambda e, tc=tc: e.dma_start(out=x_v(tc), in_=x_d[tc * 128:(tc + 1) * 128, :]),
               waits=w, inc=(sd_x, 16))
            em("sync", lambda e, tc=tc: e.dma_start(out=cs_v(tc), in_=ct_d[tc * 128:(tc + 1) * 128, :]),
               inc=(sd_cs, 16))
            em("sync", lambda e, tc=tc: e.dma_start(out=sn_v(tc), in_=st_d[tc * 128:(tc + 1) * 128, :]),
               inc=(sd_cs, 16))

        def emit_lnT(t):
            xv, xnv = x_v(t), xn_v(t)
            xtb, xtr = xtb_v(t), xtr_v(t)
            em("vector", lambda e, xv=xv: e.tensor_reduce(ln_sb[:, 0:1], xv, AX.X, OP.add),
               waits=[(sd_x, 16 * (t + 1))], inc=(sv, 1))
            sv_red = sv.n
            em("scalar", lambda e, xv=xv: e.activation(
                scrap, xv, AF.Square, scale=1.0, accum_out=ln_sb[:, 1:2]),
                waits=[(sd_x, 16 * (t + 1))], inc=(sa, 1))
            sa_sq = sa.n
            em("vector", lambda e: e.tensor_scalar(ln_sb[:, 2:3], ln_sb[:, 0:1], 1.0 / E, 0.0, OP.mult, OP.bypass),
               waits=[(sv, sv_red)], inc=(sv, 1))
            em("vector", lambda e: e.tensor_scalar(ln_sb[:, 3:4], ln_sb[:, 1:2], 1.0 / E, 0.0, OP.mult, OP.bypass),
               waits=[(sa, sa_sq)], inc=(sv, 1))
            em("vector", lambda e: e.tensor_tensor(ln_sb[:, 4:5], ln_sb[:, 2:3], ln_sb[:, 2:3], OP.mult),
               waits=[(sv, sv.n - 1)], inc=(sv, 1))
            em("vector", lambda e: e.tensor_tensor(ln_sb[:, 5:6], ln_sb[:, 3:4], ln_sb[:, 4:5], OP.subtract),
               waits=[(sv, sv.n)], inc=(sv, 1))
            em("scalar", lambda e: e.activation(ln_sb[:, 6:7], ln_sb[:, 5:6], AF.Sqrt,
                                                bias=ln_sb[:, 16:17], scale=1.0),
               waits=[(sv, sv.n)] + ([(sg, 3)] if t == 0 else []), inc=(sa, 1))
            em("vector", lambda e: e.reciprocal(ln_sb[:, 7:8], ln_sb[:, 6:7]), waits=[(sa, sa.n)], inc=(sv, 1))
            em("vector", lambda e: e.tensor_tensor(ln_sb[:, 8:9], ln_sb[:, 2:3], ln_sb[:, 7:8], OP.mult),
               waits=[(sv, sv.n)], inc=(sv, 1))
            em("vector", lambda e: e.tensor_scalar(ln_sb[:, 9:10], ln_sb[:, 8:9], -1.0, 0.0, OP.mult, OP.bypass),
               waits=[(sv, sv.n)], inc=(sv, 1))
            em("vector", lambda e, xv=xv, xnv=xnv: e.tensor_scalar(
                xnv, xv, ln_sb[:, 7:8], ln_sb[:, 9:10], OP.mult, OP.add),
               waits=[(sv, sv.n)], inc=(sv, 1))
            sv_ln = sv.n
            tw = [(sv, sv_ln)]
            if t >= 1:
                tw.append((sv, tev[t - 1]))
            if t == 0:
                tw.append((sd_s, 16))
            for ec in range(NEC):
                em("tensor", lambda e, ec=ec, xnv=xnv: e.transpose(
                    ps_c[:, ec * 128:ec * 128 + 128], xnv[:, ec * 128:ec * 128 + 128], id_sb),
                    waits=tw if ec == 0 else [], inc=(sp, 1) if ec == NEC - 1 else None)
            em("vector", lp(lambda e, xtb=xtb: e.tensor_copy(xtb, ps_c[:, 0:1024])), waits=[(sp, sp.n)],
               inc=(sv, 1))
            em("vector", lp(lambda e, xtr=xtr: e.tensor_copy(xtr, ps_c[:, 0:1024])), inc=(sv, 1))
            tev.append(sv.n)

        if NTC_:
            emit_loads(0)
            emit_loads(1)
            emit_lnT(0)
        for t in range(NTC_):
            xv, xnv, csv, snv = x_v(t), xn_v(t), cs_v(t), sn_v(t)
            xtb, xtr = xtb_v(t), xtr_v(t)
            sv_tevac = tev[t]
            for qi, (kind, half) in enumerate(
                    ((0, 0), (1, 0), (2, 0), (3, 0), (0, 1), (1, 1), (2, 1), (3, 1))):
                gseq = t * 8 + qi
                bank = ps_a[:, (gseq % 4) * 512:(gseq % 4) * 512 + 512]
                w0 = [(sv, sv_tevac)]
                if gseq >= 4:
                    w0.append((sv, qtr_ev[gseq - 4]))
                if gseq == 0:
                    w0 += [(sd_g, sd_g_wts), (sg, 4)]
                lhs = (lambda ec, xtb=xtb: xtb[:, ec * 128:ec * 128 + 128]) if kind < 3 else \
                      (lambda ec, xtr=xtr: xtr[:, ec * 128:ec * 128 + 128])
                rhs = {0: wq_t, 1: wk_t, 2: wv_t, 3: mm_t}[kind]
                for ec in range(NEC):
                    em("tensor", lambda e, lhs=lhs, rhs=rhs, ec=ec, h2=half, b=bank, s0=(ec == 0): e.matmul(
                        b, lhs(ec), rhs(ec, h2), start=s0, stop=False), waits=w0 if ec == 0 else [])
                if kind == 0:
                    em("tensor", lambda e, h2=half, b=bank: e.matmul(
                        b, onesb[0:1, 0:128], bias_a[0:1, h2 * 512:h2 * 512 + 512], start=False, stop=True),
                        inc=(sp, 1))
                elif kind == 1:
                    em("tensor", lambda e, h2=half, b=bank: e.matmul(
                        b, onesb[32:33, 0:128], bias_a[32:33, h2 * 512:h2 * 512 + 512], start=False, stop=True,
                        tile_position=(32, 0)), inc=(sp, 1))
                elif kind == 2:
                    em("tensor", lambda e, h2=half, b=bank: e.matmul(
                        b, onesb[64:65, 0:128], bias_a[64:65, h2 * 512:h2 * 512 + 512], start=False, stop=True,
                        tile_position=(64, 0)), inc=(sp, 1))
                else:
                    em("tensor", lambda e, h2=half, b=bank: e.matmul(
                        b, onesb[0:1, 0:128], onesb[0:1, 128 + h2 * 512:128 + h2 * 512 + 512],
                        start=False, stop=True), inc=(sp, 1))
                sp_q = sp.n
                if kind in (0, 1):
                    rot = (rq_v if kind == 0 else rk_v)(t)[:, half * 512:half * 512 + 512]
                    pv = bank.rearrange("p (h d) -> p h d", d=64)
                    uv = ru_v.rearrange("p (h d) -> p h d", d=64)
                    snh = snv[:, half * 512:half * 512 + 512].rearrange("p (h d) -> p h d", d=64)
                    vw0 = [(sp, sp_q), (sd_cs, 32 * (t + 1))]
                    if t >= 2:
                        vw0.append((sd_sp, 32 * (t - 1)))
                    em("vector", lp(lambda e, pv=pv, uv=uv, snh=snh: e.tensor_tensor(
                        uv[:, :, 0:32], pv[:, :, 32:64], snh[:, :, 0:32], OP.mult)), waits=vw0, inc=(sv, 1))
                    em("vector", lp(lambda e, pv=pv, uv=uv, snh=snh: e.tensor_tensor(
                        uv[:, :, 32:64], pv[:, :, 0:32], snh[:, :, 32:64], OP.mult)), inc=(sv, 1))
                    em("vector", lp(lambda e, b=bank, csv=csv, half=half: e.tensor_tensor(
                        rt_v, b, csv[:, half * 512:half * 512 + 512], OP.mult)), inc=(sv, 1))
                    em("vector", lp(lambda e, rot=rot: e.tensor_tensor(rot, ru_v, rt_v, OP.add)),
                       waits=[(sv, sv.n)], inc=(sv, 1))
                elif kind == 2:
                    vw0 = [(sp, sp_q)]
                    if gseq <= 8:
                        vw0.append((sg, 4))
                    if half == 0 and t >= 1:
                        vw0.append((sd_v, 16 * t))
                    dst = vst[:, half * 8 * VW: half * 8 * VW + 8 * VW].rearrange(
                        "p (h w) -> p h w", w=VW)[:, :, 0:64]
                    em("vector", lp(lambda e, b=bank, dst=dst: e.tensor_copy(
                        dst, b.rearrange("p (h d) -> p h d", d=64))), waits=vw0, inc=(sv, 1))
                else:
                    dst = pp_v(t)[:, half * 512:half * 512 + 512]
                    fw = [(sp, sp_q)]
                    if t >= 2:
                        fw.append((sd_pa, 16 * (t - 1)))
                    em("vector", lambda e, b=bank, dst=dst, xv=xv, half=half: e.tensor_tensor(
                        dst, b, xv[:, half * 512:half * 512 + 512], OP.add), waits=fw, inc=(sv, 1))
                qtr_ev.append(sv.n)
                if qi == 1 and t + 1 < NTC_:
                    emit_lnT(t + 1)
            if dbg:
                em("sync", lambda e, t=t: e.dma_start(out=xnd_d[t * 128:(t + 1) * 128, :], in_=xn_v(t)),
                   waits=[(sv, qtr_ev[t * 8 + 7])], inc=(sd_s, 16))
                em("sync", lambda e, t=t: e.dma_start(out=xtd_d[t, :, :], in_=xtb_v(t)),
                   waits=[(sv, qtr_ev[t * 8 + 7])], inc=(sd_s, 16))
            em("sync", lambda e, t=t: e.dma_start(out=qr_d[t * 128:(t + 1) * 128, :], in_=rq_v(t)),
               waits=[(sv, qtr_ev[t * 8 + 4])], inc=(sd_sp, 16))
            em("sync", lambda e, t=t: e.dma_start(out=kr_d[t * 128:(t + 1) * 128, :], in_=rk_v(t)),
               waits=[(sv, qtr_ev[t * 8 + 5])], inc=(sd_sp, 16))
            em("sync", lambda e, t=t: e.dma_start(out=pa_d[t * 128:(t + 1) * 128, :], in_=pp_v(t)),
               waits=[(sv, qtr_ev[t * 8 + 7])], inc=(sd_pa, 16))
            em("sync", lambda e, t=t: e.dma_start(out=v_d[t * 128:(t + 1) * 128, :], in_=vst),
               waits=[(sv, qtr_ev[t * 8 + 6])], inc=(sd_v, 16))
            tc_dve_done.append(sv.n)
            tc_sa_done.append(sa.n)
            if t + 2 < NTC:
                emit_loads(t + 2)
        sv_P_done = sv.n
        sp_P_done = sp.n

        def aot_sb(c):
            if c < 6:
                return ar2_bf[:, c * 4096:(c + 1) * 4096]
            return ar1_bf[:, (c - 6) * 4096:(c - 5) * 4096]

        qT = lambda ps0: ar1_bf[:, ps0 * 4096: ps0 * 4096 + 4096]
        kT = lambda ps0: ar1_bf[:, 16384 + ps0 * 4096: 16384 + ps0 * 4096 + 4096]

        exp_total = 0
        gqb = 0
        store_done_sv = []
        pair_sp_done = []

        def emit_vload(gp):
            w = [(sd_v, 16 * NTC), (sp, sp_B_done)] if gp < 2 else [(sp, pair_sp_done[gp - 2])]
            for s4 in range(4):
                em("sync", lambda e, c=gp, s4=s4, gp=gp: e.dma_start(
                    out=vbuf(gp)[:, s4 * 1040: s4 * 1040 + 1040].rearrange("p (c2 f) -> p c2 f", f=130),
                    in_=v_d[s4 * 1024:(s4 + 1) * 1024, c * 130:(c + 1) * 130].rearrange(
                        "(c2 p) f -> p c2 f", p=128)),
                    waits=w if s4 == 0 else [], inc=(sd_vl, 16))

        dlv = 0
        if "D" in phases:
            i_ = phases.index("D")
            dlv = int(phases[i_ + 1]) if i_ + 1 < len(phases) and phases[i_ + 1].isdigit() else 9
        DGRP = 2 if dlv >= 1 else 0
        do_exp = "X" not in phases
        if DGRP:
            emit_vload(0)
            emit_vload(1)
        for g in range(DGRP):
            if g == 0:
                w_tr = [(sd_sp, 32 * NTC), (sd_pa, 16 * NTC), (sv, sv_P_done)]
            else:
                w_tr = [(sp, sp.n)]
            for ps0 in range(4):
                em("sync", lambda e, g=g, ps0=ps0: e.dma_start(
                    out=qT(ps0), in_=qr_d[:, g * 512 + ps0 * 128: g * 512 + ps0 * 128 + 128],
                    transpose=True), waits=w_tr if ps0 == 0 else [], inc=(sd_tr, 16))
                em("sync", lambda e, g=g, ps0=ps0: e.dma_start(
                    out=kT(ps0), in_=kr_d[:, g * 512 + ps0 * 128: g * 512 + ps0 * 128 + 128],
                    transpose=True), inc=(sd_tr, 16))
            sd_tr_g = sd_tr.n
            for ps0 in range(4):
                chunk = g * 4 + ps0
                for qb in range(NQB):
                    q0 = qb * QB
                    sa_base = sa.n
                    sc_sp = []
                    abank = ps_b if gqb % 2 == 0 else ps_c

                    def emit_av(kc_av, abank=abank):
                        avw = [(sa, sa_base + kc_av + 1)]
                        if kc_av == 0 and gqb >= 2:
                            avw.append((sv, store_done_sv[gqb - 2]))
                        if kc_av == 0 and qb == 0:
                            avw.append((sd_vl, 64 * (chunk + 1)))
                        for hj in range(2):
                            em("tensor", lambda e, kc=kc_av, hj=hj, gp=chunk, abank=abank: e.matmul(
                                abank[0:65, hj * 512:hj * 512 + 512],
                                vbuf(gp)[:, kc * 130 + hj * VW: kc * 130 + hj * VW + VW],
                                ex_bf[0:128, (kc % 3) * 1024 + hj * 512:(kc % 3) * 1024 + hj * 512 + 512],
                                start=(kc == 0), stop=(kc == NKC - 1)),
                                waits=avw if hj == 0 else [], inc=(sp, 1) if hj == 1 else None)

                    for kc in range(NKC if dlv >= 2 else 0):
                        scw = []
                        if kc == 0 and qb == 0:
                            scw.append((sd_tr, sd_tr_g))
                        if do_exp and kc >= 2:
                            scw.append((sa, sa_base + kc - 1))
                        elif do_exp and gqb > 0:
                            scw.append((sa, sa_base))
                        sbank = ps_a[:, (kc % 2) * 1024:(kc % 2) * 1024 + 1024]
                        em("tensor", lambda e, ps0=ps0, kc=kc, q0=q0, b=sbank: e.matmul(
                            b[:, 0:512], kT(ps0)[0:64, kc * 128:kc * 128 + 128],
                            qT(ps0)[0:64, q0:q0 + QB], start=True, stop=True), waits=scw)
                        em("tensor", lambda e, ps0=ps0, kc=kc, q0=q0, b=sbank: e.matmul(
                            b[:, 512:1024], kT(ps0)[64:128, kc * 128:kc * 128 + 128],
                            qT(ps0)[64:128, q0:q0 + QB], start=True, stop=True,
                            tile_position=(64, 0)), inc=(sp, 1))
                        sc_sp.append(sp.n)
                        ew = [(sp, sc_sp[kc])]
                        if exp_total == 0:
                            ew.append((sg, 3))
                            ew.append((sd_v, 16 * NTC))
                        if do_exp:
                            em("scalar", lp(lambda e, kc=kc, b=sbank: e.activation(
                                ex_bf[:, (kc % 3) * 1024:(kc % 3) * 1024 + 1024], b, AF.Exp,
                                bias=ln_sb[:, 24:25], scale=0.125)), waits=ew, inc=(sa, 1))
                        exp_total += 1
                        if kc >= 1 and dlv >= 3:
                            emit_av(kc - 1)
                    if dlv >= 3:
                        emit_av(NKC - 1)
                    sp_av_last = sp.n
                    if dlv >= 4:
                        cw = [(sp, sp_av_last)]
                        if gqb >= 2:
                            cw.append((sd_ao, 16 * (gqb - 1)))
                        for hj in range(2):
                            em("vector", lp(lambda e, hj=hj, abank=abank, gqb=gqb: e.tensor_copy(
                                as_v(gqb)[hj * 64:hj * 64 + 64, :],
                                abank[0:64, hj * 512:hj * 512 + 512])),
                                waits=cw if hj == 0 else [], inc=(sv, 1))
                        rw = [(sd_rs, 16 * (gqb - 1))] if gqb >= 2 else []
                        em("vector", lp(lambda e, abank=abank, gqb=gqb: e.tensor_copy(
                            rs_v(gqb)[64:65, 0:1024], abank[64:65, 0:1024])), waits=rw, inc=(sv, 1))
                        store_done_sv.append(sv.n)
                        em("sync", lambda e, chunk=chunk, qb=qb, gqb=gqb: e.dma_start(
                            out=ao_d[chunk, :, qb * QB:(qb + 1) * QB], in_=as_v(gqb)),
                            waits=[(sv, sv.n)], inc=(sd_ao, 16))
                        em("sync", lambda e, gqb=gqb: e.dma_start(
                            out=rs_d[gqb:gqb + 1, :], in_=rs_v(gqb)[64:65, 0:1024]),
                            waits=[(sv, sv.n)], inc=(sd_rs, 16))
                    else:
                        store_done_sv.append(sv.n)
                    gqb += 1
                pair_sp_done.append(sp.n)
                if chunk + 2 < 8:
                    emit_vload(chunk + 2)
                if chunk < 6 and "E" in phases and "Y" not in phases:
                    w = [(sd_ao, 16 * NQB * (chunk + 1))]
                    if chunk == 0:
                        w.append((sp, sp_P_done))
                    em("gpsimd", lambda e, c=chunk: e.dma_start(out=aot_sb(c), in_=ao_d[c, :, :]),
                       waits=w, inc=(sd_pe, 16))
        sp_D_done = sp.n

        aot_list = ((list(range(8)) if "Y" in phases else [6, 7]) if "E" in phases else [])
        for c in aot_list:
            w = [(sd_ao, 16 * NQB * (c + 1)), (sp, sp_D_done)]
            em("sync", lambda e, c=c: e.dma_start(out=aot_sb(c), in_=ao_d[c, :, :]), waits=w, inc=(sd_pe, 16))

        rssb = ar1_bf[:, 16384:17408]
        rsrec = ar1_bf[:, 17408:18432]
        rsflat = lambda i: ar1_bf[0:1, 18432:26624]
        presc_sv = {}
        if "E" in phases and "Z" not in phases:
            em("sync", lambda e: e.dma_start(out=rssb[0:64, :], in_=rs_d),
               waits=[(sd_rs, 16 * 64), (sp, sp_D_done)], inc=(sd_rr, 16))
            em("vector", lp(lambda e: e.reciprocal(rsrec[0:64, :], rssb[0:64, :])),
               waits=[(sd_rr, 16)], inc=(sv, 1))
            em("sync", lambda e: e.dma_start(out=rsr_d, in_=rsrec[0:64, :]),
               waits=[(sv, sv.n)], inc=(sd_rr, 16))
            for c in range(8):
                fw = [(sd_rr, 32)] if c == 0 else [(sv, presc_sv[(c - 1, NQB - 1)])]
                em("sync", lambda e, c=c: e.dma_start(
                    out=rsflat(c).rearrange("o (a f) -> o a f", a=8),
                    in_=rsr_d[c * 8:(c + 1) * 8, :].unsqueeze(0)),
                    waits=fw, inc=(sd_rr, 16))
                for qb in range(NQB):
                    i = c * NQB + qb
                    dslot = ps_c[:, (i % 2) * 512:(i % 2) * 512 + 512]
                    bw = [(sd_rr, sd_rr.n)]
                    if i >= 2:
                        bw.append((sv, presc_sv[(c, qb - 2) if qb >= 2 else (c - 1, NQB - 2 + qb)]))
                    em("tensor", lambda e, c=c, qb=qb, dslot=dslot: e.matmul(
                        dslot[0:64, :], onesb[0:1, 0:64], rsflat(c)[0:1, qb * 1024: qb * 1024 + 512],
                        start=True, stop=True), waits=bw)
                    em("tensor", lambda e, c=c, qb=qb, dslot=dslot: e.matmul(
                        dslot[64:128, :], onesb[0:1, 0:64],
                        rsflat(c)[0:1, qb * 1024 + 512: qb * 1024 + 1024],
                        start=True, stop=True, tile_position=(0, 64)), inc=(sp, 1))
                    em("vector", lp(lambda e, c=c, qb=qb, dslot=dslot: e.tensor_tensor(
                        aot_sb(c)[:, qb * QB:(qb + 1) * QB],
                        aot_sb(c)[:, qb * QB:(qb + 1) * QB], dslot, OP.mult)),
                        waits=[(sp, sp.n), (sd_pe, 16 * (c + 1))], inc=(sv, 1))
                    presc_sv[(c, qb)] = sv.n

        ev_e = []

        def emit_pa(t):
            pw = [(sv, ev_e[t - 4])] if t >= 4 else [(sd_pa, 16 * NTC), (sp, sp_D_done)]
            em("sync", lambda e, t=t: e.dma_start(
                out=pa_e(t), in_=pa_d[t * 128:(t + 1) * 128, :]), waits=pw, inc=(sd_pe, 16))

        if "E" in phases:
            for t in range(4):
                emit_pa(t)
        for t in range(NTC if "E" in phases else 0):
            bank = ps_a[:, (t % 2) * 1024:(t % 2) * 1024 + 1024]
            w0 = [(sd_pe, 16 * 8 + 16 * (t + 1))]
            if t >= 2:
                w0.append((sv, ev_e[t - 2]))
            if "Z" not in phases:
                w0.append((sv, presc_sv[(7, t // 4)]))
            for c in range(8):
                em("tensor", lambda e, c=c, t=t, b=bank: e.matmul(
                    b[:, 0:512], aot_sb(c)[:, t * 128:(t + 1) * 128], wo_t(c, 0),
                    start=(c == 0), stop=False), waits=w0 if c == 0 else [])
                em("tensor", lambda e, c=c, t=t, b=bank: e.matmul(
                    b[:, 512:1024], aot_sb(c)[:, t * 128:(t + 1) * 128], wo_t(c, 1),
                    start=(c == 0), stop=(c == 7)), inc=(sp, 1) if c == 7 else None)
            ttw = [(sp, sp.n)]
            if t >= 2:
                ttw.append((sd_out, 16 * (t - 1)))
            em("vector", lambda e, t=t, b=bank: e.tensor_tensor(ou_e(t), b, pa_e(t), OP.add),
               waits=ttw, inc=(sv, 1))
            ev_e.append(sv.n)
            em("sync", lambda e, t=t: e.dma_start(out=out_d[t * 128:(t + 1) * 128, :], in_=ou_e(t)),
               waits=[(sv, sv.n)], inc=(sd_out, 16))
            if t + 4 < NTC:
                emit_pa(t + 4)
        if "E" in phases:
            em("sync", lambda e: None, waits=[(sd_out, sd_out.n)])

        nc._prog_debug = (prog, sems)
        block = stk.enter_context(nc.Block())
        sections = {"sync": block.sync, "vector": block.vector, "scalar": block.scalar,
                    "tensor": block.tensor, "gpsimd": block.gpsimd}

        def emitter(name):
            def body(eng):
                for (waits, fn, inc) in prog[name]:
                    for (sem, val) in waits:
                        if val > 0:
                            eng.wait_ge(sem.h, val)
                    inst = fn(eng)
                    if inc is not None and inst is not None:
                        inst.then_inc(inc[0].h, inc[1])
            return body

        for name, section in sections.items():
            section(emitter(name))
    return nc


def kernel(L0_attn_x, L0_attn_wq, L0_attn_bq, L0_attn_wk, L0_attn_bk,
           L0_attn_wv, L0_attn_bv, L0_attn_wo, L0_attn_bo,
           L0_ffn_w_freq_real, L0_ffn_w_freq_imag, L0_ffn_bias):
    f32 = np.float32
    bf16 = ml_dtypes.bfloat16
    x = np.asarray(L0_attn_x, f32)
    wq = np.asarray(L0_attn_wq, f32); bq = np.asarray(L0_attn_bq, f32)
    wk = np.asarray(L0_attn_wk, f32); bk = np.asarray(L0_attn_bk, f32)
    wv = np.asarray(L0_attn_wv, f32); bv = np.asarray(L0_attn_bv, f32)
    wo = np.asarray(L0_attn_wo, f32); bo = np.asarray(L0_attn_bo, f32)
    wr = np.asarray(L0_ffn_w_freq_real, f32)
    wi = np.asarray(L0_ffn_w_freq_imag, f32)
    bffn = np.asarray(L0_ffn_bias, f32)

    idx = (np.outer(np.arange(E, dtype=np.int64), np.arange(E, dtype=np.int64)) % E)
    ang = (2.0 * np.pi / E) * idx.astype(np.float64)
    cmat = np.cos(ang)
    smat = np.sin(ang)
    ch = cmat.astype(bf16)
    cl = (cmat - ch.astype(np.float64)).astype(bf16)
    sh = smat.astype(bf16)
    sl = (smat - sh.astype(np.float64)).astype(bf16)
    inv_freq = 1.0 / 10000.0 ** (np.arange(0, HD, 2, dtype=np.float64) / HD)
    freqs = np.outer(np.arange(S, dtype=np.float64), inv_freq)
    cos64 = np.repeat(np.cos(freqs), 2, axis=1)
    sin64 = np.repeat(np.sin(freqs), 2, axis=1)
    sin64[:, :32] *= -1.0
    cost = np.ascontiguousarray(np.tile(cos64, (1, H)).astype(f32))
    sint = np.ascontiguousarray(np.tile(sin64, (1, H)).astype(f32))
    ident = np.eye(128, dtype=f32)

    common = {
        "wqT": np.ascontiguousarray(wq.T), "wkT": np.ascontiguousarray(wk.T),
        "wvT": np.ascontiguousarray(wv.T), "woT": np.ascontiguousarray(wo.T),
        "bq": bq.reshape(1, E), "bk": bk.reshape(1, E), "bv": bv.reshape(1, E),
        "bo": bo.reshape(1, E), "bffn": bffn.reshape(1, E),
        "wfr": np.ascontiguousarray(wr), "wfi": np.ascontiguousarray(wi),
        "ch": ch, "cl": cl, "sh": sh, "sl": sl,
        "cost": cost, "sint": sint, "ident": ident,
    }
    in_maps = []
    for c in range(NCORES):
        m = dict(common)
        m["x"] = np.ascontiguousarray(x[c])
        in_maps.append(m)

    nc = build_nc()
    res = run_bass_kernel_spmd(nc, in_maps, core_ids=list(range(NCORES)))
    return np.stack([np.asarray(r["out"], f32) for r in res.results], axis=0)


# version 20
# speedup vs baseline: 1.0595x; 1.0295x over previous
import numpy as np
import ml_dtypes
from contextlib import ExitStack
import concourse.bass as bass
import concourse.mybir as mybir
from concourse.bass_utils import run_bass_kernel_spmd
from concourse.alu_op_type import AluOpType as OP

F32 = mybir.dt.float32
F32R = mybir.dt.float32r
BF16 = mybir.dt.bfloat16
AF = mybir.ActivationFunctionType
AX = mybir.AxisListType

S, E, H, HD = 4096, 1024, 16, 64
NTC = S // 128
NEC = E // 128
SHIFT = 12.0
EPS = 1e-5
NCORES = 8
VW = 65
QB = 512
NQB = S // QB
NKC = NTC


class Sem:
    def __init__(self, h):
        self.h = h
        self.n = 0

    def inc(self, k=1):
        self.n += k
        return self.n


def build_nc(phases="BGPDE", dbg=False):
    nc = bass.Bass("TRN2", target_bir_lowering=False, debug=False)
    DT = nc.dram_tensor
    x_d = DT("x", [S, E], F32, kind="ExternalInput").ap()
    wq_d = DT("wqT", [E, E], F32, kind="ExternalInput").ap()
    wk_d = DT("wkT", [E, E], F32, kind="ExternalInput").ap()
    wv_d = DT("wvT", [E, E], F32, kind="ExternalInput").ap()
    wo_d = DT("woT", [E, E], F32, kind="ExternalInput").ap()
    bq_d = DT("bq", [1, E], F32, kind="ExternalInput").ap()
    bk_d = DT("bk", [1, E], F32, kind="ExternalInput").ap()
    bv_d = DT("bv", [1, E], F32, kind="ExternalInput").ap()
    bo_d = DT("bo", [1, E], F32, kind="ExternalInput").ap()
    bf_d = DT("bffn", [1, E], F32, kind="ExternalInput").ap()
    wr_d = DT("wfr", [E, E], F32, kind="ExternalInput").ap()
    wi_d = DT("wfi", [E, E], F32, kind="ExternalInput").ap()
    ch_d = DT("ch", [E, E], BF16, kind="ExternalInput").ap()
    cl_d = DT("cl", [E, E], BF16, kind="ExternalInput").ap()
    sh_d = DT("sh", [E, E], BF16, kind="ExternalInput").ap()
    sl_d = DT("sl", [E, E], BF16, kind="ExternalInput").ap()
    ct_d = DT("cost", [S, E], F32, kind="ExternalInput").ap()
    st_d = DT("sint", [S, E], F32, kind="ExternalInput").ap()
    id_d = DT("ident", [128, 128], F32, kind="ExternalInput").ap()
    out_d = DT("out", [S, E], F32, kind="ExternalOutput").ap()
    KI = "ExternalOutput" if dbg else "Internal"
    qr_d = DT("qrot", [S, E], BF16, kind=KI).ap()
    kr_d = DT("krot", [S, E], BF16, kind=KI).ap()
    pa_d = DT("partial", [S, E], F32, kind=KI).ap()
    ao_d = DT("attnOT", [8, 128, S], BF16, kind=KI).ap()
    v_d = DT("vaug", [S, 16 * VW], BF16, kind=KI).ap()
    rs_d = DT("rowsums", [64, 1024], BF16, kind="Internal").ap()
    rsr_d = DT("rowrecip", [64, 1024], BF16, kind="Internal").ap()
    md_d = DT("madump", [E, E], F32, kind="ExternalOutput").ap() if dbg else None
    xnd_d = DT("xndump", [S, E], F32, kind="ExternalOutput").ap() if dbg else None
    xtd_d = DT("xtdump", [NTC, 128, E], BF16, kind="ExternalOutput").ap() if dbg else None

    with ExitStack() as stk:
        EN = stk.enter_context
        ar1 = EN(nc.sbuf_tensor("ar1", [128, 16384], F32))
        ar2 = EN(nc.sbuf_tensor("ar2", [128, 16384], F32))
        m_r = EN(nc.sbuf_tensor("m_r", [128, 8192], F32R))
        xnt_r = EN(nc.sbuf_tensor("xnt_r", [128, 2048], F32R))
        vsl = EN(nc.sbuf_tensor("vsl", [128, 8320], BF16))
        ar4 = EN(nc.sbuf_tensor("ar4", [128, 3072], F32))
        bias_a = EN(nc.sbuf_tensor("bias_a", [65, 1024], BF16))
        onesb = EN(nc.sbuf_tensor("onesb", [65, 1152], BF16))
        ln_sb = EN(nc.sbuf_tensor("ln_sb", [128, 32], F32))
        ps_a = EN(nc.psum_tensor("ps_a", [128, 2048], F32))
        ps_b = EN(nc.psum_tensor("ps_b", [128, 1024], F32))
        ps_c = EN(nc.psum_tensor("ps_c", [128, 1024], F32))

        sems = {}
        for name in ("sp", "sv", "sa", "sg", "sd_x", "sd_cs", "sd_s", "sd_g",
                     "sd_sp", "sd_pa", "sd_tr", "sd_ao", "sd_pe", "sd_out",
                     "sd_v", "sd_vl", "sd_rs", "sd_rr"):
            sems[name] = Sem(EN(nc.semaphore(name)))
        sp, sv, sa, sg = sems["sp"], sems["sv"], sems["sa"], sems["sg"]
        sd_x, sd_cs, sd_s, sd_g = sems["sd_x"], sems["sd_cs"], sems["sd_s"], sems["sd_g"]
        sd_sp, sd_pa, sd_tr, sd_ao = sems["sd_sp"], sems["sd_pa"], sems["sd_tr"], sems["sd_ao"]
        sd_pe, sd_out = sems["sd_pe"], sems["sd_out"]
        sd_v, sd_vl = sems["sd_v"], sems["sd_vl"]
        sd_rs, sd_rr = sems["sd_rs"], sems["sd_rr"]

        prog = {k: [] for k in ("tensor", "vector", "scalar", "sync", "gpsimd")}

        def em(eng, fn, waits=(), inc=None):
            if inc is not None:
                inc[0].inc(inc[1])
            prog[eng].append((tuple(waits), fn, inc))

        def lp(f):
            def g(e):
                with nc.allow_low_precision(reason="intentional bf16/f32r rounding"):
                    return f(e)
            return g

        ar1_bf = ar1[:].bitcast(BF16)
        ar2_bf = ar2[:].bitcast(BF16)

        x_v = lambda t: ar1[:, (t % 2) * E: (t % 2) * E + E]
        xn_v = lambda t: ar1[:, 2048 + (t % 2) * E: 2048 + (t % 2) * E + E]
        cs_v = lambda t: ar1[:, 4096 + (t % 2) * E: 4096 + (t % 2) * E + E]
        sn_v = lambda t: ar1[:, 6144 + (t % 2) * E: 6144 + (t % 2) * E + E]
        pp_v = lambda t: ar1[:, 8192 + (t % 2) * E: 8192 + (t % 2) * E + E]
        xtb_v = lambda t: ar1_bf[:, 20480 + (t % 2) * E: 20480 + (t % 2) * E + E]
        rq_v = lambda t: ar1_bf[:, 22528 + (t % 2) * E: 22528 + (t % 2) * E + E]
        rk_v = lambda t: ar1_bf[:, 24576 + (t % 2) * E: 24576 + (t % 2) * E + E]
        ru_v = ar1_bf[:, 26624:27136]
        rt_v = ar1_bf[:, 27136:27648]
        xtr_v = lambda t: xnt_r[:, (t % 2) * E: (t % 2) * E + E]
        pa_e = lambda t: ar1[:, 4096 + (t % 4) * 1024: 4096 + (t % 4) * 1024 + 1024]
        ou_e = lambda t: ar1[:, 13312 + (t % 2) * 1024: 13312 + (t % 2) * 1024 + 1024]

        stgin = ar4[:, 0:1024]
        stg2 = ar4[:, 1536:2560]
        stg_i = lambda i: (stgin, stg2)[i % 2]
        scrap = ar4[:, 0:1024]
        tmp1 = ar4[:, 1024:2048]
        tA = ar4[:, 1024:1152]
        tB = ar4[:, 1152:1280]
        tC = ar4[:, 1280:1408]
        m_part = ar4[:, 1536:2560]
        id_sb = ar4[:, 2560:2688]
        ex_bf = ar4[:, 0:1536].bitcast(BF16)
        as_v = lambda i: ar4[:, 1536 + (i % 2) * 256: 1536 + (i % 2) * 256 + 256].bitcast(BF16)
        rs_v = lambda i: ar4[:, 2048 + (i % 2) * 512: 2048 + (i % 2) * 512 + 512].bitcast(BF16)
        vst = ar4[:, 1024:1544].bitcast(BF16)

        chf = lambda jc, oh: ar1_bf[:, jc * 1024 + oh * 512: jc * 1024 + oh * 512 + 512]
        clf = lambda jc, oh: ar1_bf[:, 8192 + jc * 1024 + oh * 512: 8192 + jc * 1024 + oh * 512 + 512]
        shf = lambda jc, oh: ar1_bf[:, 16384 + jc * 1024 + oh * 512: 16384 + jc * 1024 + oh * 512 + 512]
        slf = lambda jc, oh: ar1_bf[:, 24576 + jc * 1024 + oh * 512: 24576 + jc * 1024 + oh * 512 + 512]
        wrh = lambda kc, ac: ar2_bf[:, kc * 1024 + ac * 128: kc * 1024 + ac * 128 + 128]
        wrl = lambda kc, ac: ar2_bf[:, 8192 + kc * 1024 + ac * 128: 8192 + kc * 1024 + ac * 128 + 128]
        wih = lambda kc, ac: ar2_bf[:, 16384 + kc * 1024 + ac * 128: 16384 + kc * 1024 + ac * 128 + 128]
        wil = lambda kc, ac: ar2_bf[:, 24576 + kc * 1024 + ac * 128: 24576 + kc * 1024 + ac * 128 + 128]
        t1h = vsl[:, 0:1024]
        t1l = vsl[:, 1024:2048]
        t2h = vsl[:, 2048:3072]
        t2l = vsl[:, 3072:4096]
        rsl = lambda base, kc, rc: ar1_bf[:, base + kc * 1024 + rc * 128: base + kc * 1024 + rc * 128 + 128]
        vbuf = lambda i: vsl[:, (i % 2) * 4160: (i % 2) * 4160 + 4160]

        em("gpsimd", lambda e: e.memset(onesb[:], 1.0), inc=(sg, 1))
        em("gpsimd", lambda e: e.memset(ln_sb[:, 16:17], EPS), inc=(sg, 1))
        em("gpsimd", lambda e: e.memset(ln_sb[:, 24:25], -SHIFT), inc=(sg, 1))
        em("sync", lambda e: e.dma_start(out=id_sb, in_=id_d), inc=(sd_s, 16))
        em("gpsimd", lambda e: e.dma_start(out=bias_a[0:1, :], in_=bq_d), inc=(sd_g, 16))
        em("gpsimd", lambda e: e.dma_start(out=bias_a[32:33, :], in_=bk_d), inc=(sd_g, 16))
        em("gpsimd", lambda e: e.dma_start(out=bias_a[64:65, :], in_=bv_d), inc=(sd_g, 16))
        em("sync", lambda e: e.dma_start(out=tmp1[0:1, :], in_=bo_d), inc=(sd_s, 16))
        em("sync", lambda e: e.dma_start(out=scrap[0:1, :], in_=bf_d), inc=(sd_s, 16))
        em("vector", lp(lambda e: e.tensor_tensor(onesb[0:1, 128:1152], tmp1[0:1, :], scrap[0:1, :], OP.add)),
           waits=[(sd_s, sd_s.n), (sg, 1)], inc=(sv, 1))
        sv_bias = sv.n

        for src, off in ((ch_d, 0), (cl_d, 8192), (sh_d, 16384), (sl_d, 24576)):
            em("sync", lambda e, src=src, off=off: e.dma_start(
                out=ar1_bf[:, off:off + 8192].rearrange("p (c f) -> p c f", f=1024),
                in_=src.rearrange("(c p) f -> p c f", p=128)), inc=(sd_s, 16))
        sd_s_b = sd_s.n
        wsplit_sv = [sv_bias]
        wi_ctr = 0
        for i, (wsrc, offh, offl) in enumerate(((wr_d, 0, 8192), (wi_d, 16384, 24576))):
            for c in range(NEC):
                j = wi_ctr
                wi_ctr += 1
                stg = stg_i(j)
                w = [(sv, sv_bias)] if j < 2 else [(sv, wsplit_sv[j - 1])]
                em("gpsimd", lambda e, wsrc=wsrc, c=c, stg=stg: e.dma_start(
                    out=stg, in_=wsrc[c * 128:(c + 1) * 128, :]), waits=w, inc=(sd_g, 16))
                hi = ar2_bf[:, offh + c * 1024: offh + c * 1024 + 1024]
                lo = ar2_bf[:, offl + c * 1024: offl + c * 1024 + 1024]
                em("vector", lp(lambda e, hi=hi, stg=stg: e.tensor_copy(hi, stg)),
                   waits=[(sd_g, sd_g.n)], inc=(sv, 1))
                wtmp = vsl[:].bitcast(F32)[:, 0:1024]
                em("vector", lambda e, hi=hi, wtmp=wtmp: e.tensor_copy(wtmp, hi),
                   waits=[(sv, sv.n)], inc=(sv, 1))
                em("vector", lp(lambda e, lo=lo, stg=stg, wtmp=wtmp: e.tensor_tensor(
                    lo, stg, wtmp, OP.subtract)),
                   waits=[(sv, sv.n)], inc=(sv, 1))
                wsplit_sv.append(sv.n)
        sv_wsplit = sv.n

        CH, CL, SH, SL = 0, 8192, 16384, 24576
        st1a_pairs = ((wrh, CH), (wrl, CH), (wrh, CL), (wih, SH), (wil, SH), (wih, SL))
        g1 = ((wih, CH), (wil, CH), (wih, CL))
        g2 = ((wrh, SH), (wrl, SH), (wrh, SL))
        st2 = ((t1h, chf), (t1l, chf), (t1h, clf), (t2h, shf), (t2l, shf), (t2h, slf))
        t1ev_prev = t2ev_prev = None
        mpart_ev = {}
        dump_dma = 0
        for rc in range(NEC):
            t1ev = []
            for ac in range(NEC):
                bank = ps_a[:, (ac % 2) * 512:(ac % 2) * 512 + 128]
                if ac == 0 and rc == 0:
                    w0 = [(sv, sv_wsplit), (sd_s, sd_s_b)]
                elif ac < 2:
                    w0 = [(sv, t1ev_prev[6 + ac])] if rc else []
                else:
                    w0 = [(sv, t1ev[ac - 2])]
                first = True
                k = 0
                for kc in range(NEC):
                    for (lt, rbase) in st1a_pairs:
                        k += 1
                        last = k == len(st1a_pairs) * NEC
                        em("tensor", lambda e, lt=lt, rb=rbase, kc=kc, ac=ac, rc=rc, b=bank, s0=first, s1=last:
                           e.matmul(b, lt(kc, ac), rsl(rb, kc, rc), start=s0, stop=s1),
                           waits=w0 if first else [], inc=(sp, 1) if last else None)
                        first = False
                        w0 = []
                hs = t1h[:, ac * 128:ac * 128 + 128]
                ls = t1l[:, ac * 128:ac * 128 + 128]
                em("vector", lambda e, b=bank: e.tensor_scalar(tA, b, 1.0 / E, 0.0, OP.mult, OP.bypass),
                   waits=[(sp, sp.n)], inc=(sv, 1))
                em("vector", lp(lambda e, hs=hs: e.tensor_copy(hs, tA)), waits=[(sv, sv.n)], inc=(sv, 1))
                em("vector", lambda e, hs=hs: e.tensor_copy(tB, hs), waits=[(sv, sv.n)], inc=(sv, 1))
                em("vector", lp(lambda e, ls=ls: e.tensor_tensor(ls, tA, tB, OP.subtract)),
                   waits=[(sv, sv.n)], inc=(sv, 1))
                t1ev.append(sv.n)
            t2ev = []
            for ac in range(NEC):
                bA = ps_a[:, 1024 + (ac % 2) * 512: 1024 + (ac % 2) * 512 + 128]
                bB = ps_c[:, (ac % 2) * 512:(ac % 2) * 512 + 128]
                if ac < 2:
                    w0 = [(sv, t2ev_prev[6 + ac])] if rc else []
                else:
                    w0 = [(sv, t2ev[ac - 2])]
                for gi, (bk_, pairs) in enumerate(((bA, g1), (bB, g2))):
                    first = True
                    k = 0
                    for kc in range(NEC):
                        for (lt, rbase) in pairs:
                            k += 1
                            last = k == len(pairs) * NEC
                            em("tensor", lambda e, lt=lt, rb=rbase, kc=kc, ac=ac, rc=rc, b=bk_, s0=first, s1=last:
                               e.matmul(b, lt(kc, ac), rsl(rb, kc, rc), start=s0, stop=s1),
                               waits=w0 if (first and gi == 0) else [],
                               inc=(sp, 1) if (last and gi == 1) else None)
                            first = False
                hs = t2h[:, ac * 128:ac * 128 + 128]
                ls = t2l[:, ac * 128:ac * 128 + 128]
                em("vector", lambda e, b=bA: e.tensor_scalar(tA, b, 1.0 / E, 0.0, OP.mult, OP.bypass),
                   waits=[(sp, sp.n)], inc=(sv, 1))
                em("vector", lambda e, b=bB: e.tensor_scalar(tB, b, 1.0 / E, 0.0, OP.mult, OP.bypass),
                   inc=(sv, 1))
                em("vector", lambda e: e.tensor_tensor(tC, tB, tA, OP.subtract), waits=[(sv, sv.n)], inc=(sv, 1))
                em("vector", lp(lambda e, hs=hs: e.tensor_copy(hs, tC)), waits=[(sv, sv.n)], inc=(sv, 1))
                em("vector", lambda e, hs=hs: e.tensor_copy(tA, hs), waits=[(sv, sv.n)], inc=(sv, 1))
                em("vector", lp(lambda e, ls=ls: e.tensor_tensor(ls, tC, tA, OP.subtract)),
                   waits=[(sv, sv.n)], inc=(sv, 1))
                t2ev.append(sv.n)
            sv_stage1 = sv.n
            for oh in range(2):
                bank = ps_b[:, oh * 512: oh * 512 + 512]
                w0 = [(sv, sv_stage1)] if oh == 0 else []
                if rc:
                    w0 = w0 + [(sv, mpart_ev[(rc - 1, oh)])]
                first = True
                k = 0
                for jc in range(NEC):
                    for (lt, rt) in st2:
                        k += 1
                        last = k == len(st2) * NEC
                        em("tensor", lambda e, lt=lt, rt=rt, jc=jc, oh=oh, b=bank, s0=first, s1=last: e.matmul(
                            b, lt[:, jc * 128:jc * 128 + 128], rt(jc, oh), start=s0, stop=s1),
                            waits=w0 if first else [], inc=(sp, 1) if last else None)
                        first = False
                        w0 = []
                mw = [(sp, sp.n)]
                if dbg and rc:
                    mw.append((sd_s, dump_dma))
                em("vector", lp(lambda e, rc=rc, oh=oh, b=bank: e.tensor_copy(
                    m_r[:, rc * 1024 + oh * 512: rc * 1024 + oh * 512 + 512], b)),
                    waits=mw, inc=(sv, 1))
                if dbg:
                    em("vector", lambda e, oh=oh, b=bank: e.tensor_copy(
                        m_part[:, oh * 512:oh * 512 + 512], b), inc=(sv, 1))
                mpart_ev[(rc, oh)] = sv.n
            if dbg:
                em("sync", lambda e, rc=rc: e.dma_start(out=md_d[rc * 128:(rc + 1) * 128, :], in_=m_part),
                   waits=[(sv, mpart_ev[(rc, 1)])], inc=(sd_s, 16))
                dump_dma = sd_s.n
            t1ev_prev, t2ev_prev = t1ev, t2ev
        sp_B_done = sp.n
        sv_B_done = sv.n

        vst_ones = vst.rearrange("p (n w) -> p n w", w=VW)[:, :, 64:65]
        em("gpsimd", lambda e: e.memset(vst_ones, 1.0), waits=[(sv, sv_B_done), (sp, sp_B_done)], inc=(sg, 1))

        for wdd, off in ((wq_d, 0), (wk_d, 8192), (wv_d, 16384), (wo_d, 24576)):
            em("gpsimd", lambda e, wdd=wdd, off=off: e.dma_start(
                out=ar2_bf[:, off:off + 8192].rearrange("p (c f) -> p c f", f=1024),
                in_=wdd.rearrange("(c p) f -> p c f", p=128)),
                waits=[(sp, sp_B_done)] if off == 0 else [], inc=(sd_g, 16))
        sd_g_wts = sd_g.n
        wq_t = lambda ec, h2: ar2_bf[:, ec * 1024 + h2 * 512: ec * 1024 + h2 * 512 + 512]
        wk_t = lambda ec, h2: ar2_bf[:, 8192 + ec * 1024 + h2 * 512: 8192 + ec * 1024 + h2 * 512 + 512]
        wv_t = lambda ec, h2: ar2_bf[:, 16384 + ec * 1024 + h2 * 512: 16384 + ec * 1024 + h2 * 512 + 512]
        wo_t = lambda c, h2: ar2_bf[:, 24576 + c * 1024 + h2 * 512: 24576 + c * 1024 + h2 * 512 + 512]
        mm_t = lambda ec, h2: m_r[:, ec * 1024 + h2 * 512: ec * 1024 + h2 * 512 + 512]

        if "P" not in phases:
            NTC_ = 0
        else:
            NTC_ = NTC
        tc_dve_done = []
        tc_sa_done = []
        tev = []
        lnv = []
        qtr_ev = []

        def emit_loads(tc):
            w = [(sv, tc_dve_done[tc - 2]), (sa, tc_sa_done[tc - 2])] if tc >= 2 else \
                [(sp, sp_B_done), (sv, sv_B_done)]
            em("sync", lambda e, tc=tc: e.dma_start(out=x_v(tc), in_=x_d[tc * 128:(tc + 1) * 128, :]),
               waits=w, inc=(sd_x, 16))
            em("sync", lambda e, tc=tc: e.dma_start(out=cs_v(tc), in_=ct_d[tc * 128:(tc + 1) * 128, :]),
               inc=(sd_cs, 16))
            em("sync", lambda e, tc=tc: e.dma_start(out=sn_v(tc), in_=st_d[tc * 128:(tc + 1) * 128, :]),
               inc=(sd_cs, 16))

        def emit_ln(t):
            xv, xnv = x_v(t), xn_v(t)
            em("vector", lambda e, xv=xv: e.tensor_reduce(ln_sb[:, 0:1], xv, AX.X, OP.add),
               waits=[(sd_x, 16 * (t + 1))], inc=(sv, 1))
            sv_red = sv.n
            em("scalar", lambda e, xv=xv: e.activation(
                scrap, xv, AF.Square, scale=1.0, accum_out=ln_sb[:, 1:2]),
                waits=[(sd_x, 16 * (t + 1))], inc=(sa, 1))
            sa_sq = sa.n
            em("vector", lambda e: e.tensor_scalar(ln_sb[:, 2:3], ln_sb[:, 0:1], 1.0 / E, 0.0, OP.mult, OP.bypass),
               waits=[(sv, sv_red)], inc=(sv, 1))
            em("vector", lambda e: e.tensor_scalar(ln_sb[:, 3:4], ln_sb[:, 1:2], 1.0 / E, 0.0, OP.mult, OP.bypass),
               waits=[(sa, sa_sq)], inc=(sv, 1))
            em("vector", lambda e: e.tensor_tensor(ln_sb[:, 4:5], ln_sb[:, 2:3], ln_sb[:, 2:3], OP.mult),
               waits=[(sv, sv.n - 1)], inc=(sv, 1))
            em("vector", lambda e: e.tensor_tensor(ln_sb[:, 5:6], ln_sb[:, 3:4], ln_sb[:, 4:5], OP.subtract),
               waits=[(sv, sv.n)], inc=(sv, 1))
            em("scalar", lambda e: e.activation(ln_sb[:, 6:7], ln_sb[:, 5:6], AF.Sqrt,
                                                bias=ln_sb[:, 16:17], scale=1.0),
               waits=[(sv, sv.n)] + ([(sg, 3)] if t == 0 else []), inc=(sa, 1))
            em("vector", lambda e: e.reciprocal(ln_sb[:, 7:8], ln_sb[:, 6:7]), waits=[(sa, sa.n)], inc=(sv, 1))
            em("vector", lambda e: e.tensor_tensor(ln_sb[:, 8:9], ln_sb[:, 2:3], ln_sb[:, 7:8], OP.mult),
               waits=[(sv, sv.n)], inc=(sv, 1))
            em("vector", lambda e: e.tensor_scalar(ln_sb[:, 9:10], ln_sb[:, 8:9], -1.0, 0.0, OP.mult, OP.bypass),
               waits=[(sv, sv.n)], inc=(sv, 1))
            em("vector", lambda e, xv=xv, xnv=xnv: e.tensor_scalar(
                xnv, xv, ln_sb[:, 7:8], ln_sb[:, 9:10], OP.mult, OP.add),
               waits=[(sv, sv.n)], inc=(sv, 1))
            lnv.append(sv.n)

        def emit_tr(t):
            xnv = xn_v(t)
            xtb, xtr = xtb_v(t), xtr_v(t)
            tw = [(sv, lnv[t])]
            if t >= 1:
                tw.append((sv, tev[t - 1]))
            if t == 0:
                tw.append((sd_s, 16))
            for ec in range(NEC):
                em("tensor", lambda e, ec=ec, xnv=xnv: e.transpose(
                    ps_c[:, ec * 128:ec * 128 + 128], xnv[:, ec * 128:ec * 128 + 128], id_sb),
                    waits=tw if ec == 0 else [], inc=(sp, 1) if ec == NEC - 1 else None)
            em("vector", lp(lambda e, xtb=xtb: e.tensor_copy(xtb, ps_c[:, 0:1024])), waits=[(sp, sp.n)],
               inc=(sv, 1))
            em("vector", lp(lambda e, xtr=xtr: e.tensor_copy(xtr, ps_c[:, 0:1024])), inc=(sv, 1))
            tev.append(sv.n)

        if NTC_:
            emit_loads(0)
            emit_loads(1)
            emit_ln(0)
            emit_tr(0)
        for t in range(NTC_):
            xv, xnv, csv, snv = x_v(t), xn_v(t), cs_v(t), sn_v(t)
            xtb, xtr = xtb_v(t), xtr_v(t)
            sv_tevac = tev[t]
            for qi, (kind, half) in enumerate(
                    ((0, 0), (1, 0), (2, 0), (3, 0), (0, 1), (1, 1), (2, 1), (3, 1))):
                gseq = t * 8 + qi
                bank = ps_a[:, (gseq % 4) * 512:(gseq % 4) * 512 + 512]
                w0 = [(sv, sv_tevac)]
                if gseq >= 4:
                    w0.append((sv, qtr_ev[gseq - 4]))
                if gseq == 0:
                    w0 += [(sd_g, sd_g_wts), (sg, 4)]
                lhs = (lambda ec, xtb=xtb: xtb[:, ec * 128:ec * 128 + 128]) if kind < 3 else \
                      (lambda ec, xtr=xtr: xtr[:, ec * 128:ec * 128 + 128])
                rhs = {0: wq_t, 1: wk_t, 2: wv_t, 3: mm_t}[kind]
                for ec in range(NEC):
                    em("tensor", lambda e, lhs=lhs, rhs=rhs, ec=ec, h2=half, b=bank, s0=(ec == 0): e.matmul(
                        b, lhs(ec), rhs(ec, h2), start=s0, stop=False), waits=w0 if ec == 0 else [])
                if kind == 0:
                    em("tensor", lambda e, h2=half, b=bank: e.matmul(
                        b, onesb[0:1, 0:128], bias_a[0:1, h2 * 512:h2 * 512 + 512], start=False, stop=True),
                        inc=(sp, 1))
                elif kind == 1:
                    em("tensor", lambda e, h2=half, b=bank: e.matmul(
                        b, onesb[32:33, 0:128], bias_a[32:33, h2 * 512:h2 * 512 + 512], start=False, stop=True,
                        tile_position=(32, 0)), inc=(sp, 1))
                elif kind == 2:
                    em("tensor", lambda e, h2=half, b=bank: e.matmul(
                        b, onesb[64:65, 0:128], bias_a[64:65, h2 * 512:h2 * 512 + 512], start=False, stop=True,
                        tile_position=(64, 0)), inc=(sp, 1))
                else:
                    em("tensor", lambda e, h2=half, b=bank: e.matmul(
                        b, onesb[0:1, 0:128], onesb[0:1, 128 + h2 * 512:128 + h2 * 512 + 512],
                        start=False, stop=True), inc=(sp, 1))
                sp_q = sp.n
                if kind in (0, 1):
                    rot = (rq_v if kind == 0 else rk_v)(t)[:, half * 512:half * 512 + 512]
                    pv = bank.rearrange("p (h d) -> p h d", d=64)
                    uv = ru_v.rearrange("p (h d) -> p h d", d=64)
                    snh = snv[:, half * 512:half * 512 + 512].rearrange("p (h d) -> p h d", d=64)
                    vw0 = [(sp, sp_q), (sd_cs, 32 * (t + 1))]
                    if t >= 2:
                        vw0.append((sd_sp, 32 * (t - 1)))
                    em("vector", lp(lambda e, pv=pv, uv=uv, snh=snh: e.tensor_tensor(
                        uv[:, :, 0:32], pv[:, :, 32:64], snh[:, :, 0:32], OP.mult)), waits=vw0, inc=(sv, 1))
                    em("vector", lp(lambda e, pv=pv, uv=uv, snh=snh: e.tensor_tensor(
                        uv[:, :, 32:64], pv[:, :, 0:32], snh[:, :, 32:64], OP.mult)), inc=(sv, 1))
                    em("vector", lp(lambda e, b=bank, csv=csv, half=half: e.tensor_tensor(
                        rt_v, b, csv[:, half * 512:half * 512 + 512], OP.mult)), inc=(sv, 1))
                    em("vector", lp(lambda e, rot=rot: e.tensor_tensor(rot, ru_v, rt_v, OP.add)),
                       waits=[(sv, sv.n)], inc=(sv, 1))
                elif kind == 2:
                    vw0 = [(sp, sp_q)]
                    if gseq <= 8:
                        vw0.append((sg, 4))
                    if half == 0 and t >= 1:
                        vw0.append((sd_v, 16 * t))
                    dst = vst[:, half * 8 * VW: half * 8 * VW + 8 * VW].rearrange(
                        "p (h w) -> p h w", w=VW)[:, :, 0:64]
                    em("vector", lp(lambda e, b=bank, dst=dst: e.tensor_copy(
                        dst, b.rearrange("p (h d) -> p h d", d=64))), waits=vw0, inc=(sv, 1))
                else:
                    dst = pp_v(t)[:, half * 512:half * 512 + 512]
                    fw = [(sp, sp_q)]
                    if t >= 2:
                        fw.append((sd_pa, 16 * (t - 1)))
                    em("vector", lambda e, b=bank, dst=dst, xv=xv, half=half: e.tensor_tensor(
                        dst, b, xv[:, half * 512:half * 512 + 512], OP.add), waits=fw, inc=(sv, 1))
                qtr_ev.append(sv.n)
                if qi == 0 and t + 1 < NTC_:
                    emit_ln(t + 1)
                if qi == 3 and t + 1 < NTC_:
                    emit_tr(t + 1)
            if dbg:
                em("sync", lambda e, t=t: e.dma_start(out=xnd_d[t * 128:(t + 1) * 128, :], in_=xn_v(t)),
                   waits=[(sv, qtr_ev[t * 8 + 7])], inc=(sd_s, 16))
                em("sync", lambda e, t=t: e.dma_start(out=xtd_d[t, :, :], in_=xtb_v(t)),
                   waits=[(sv, qtr_ev[t * 8 + 7])], inc=(sd_s, 16))
            em("sync", lambda e, t=t: e.dma_start(out=qr_d[t * 128:(t + 1) * 128, :], in_=rq_v(t)),
               waits=[(sv, qtr_ev[t * 8 + 4])], inc=(sd_sp, 16))
            em("sync", lambda e, t=t: e.dma_start(out=kr_d[t * 128:(t + 1) * 128, :], in_=rk_v(t)),
               waits=[(sv, qtr_ev[t * 8 + 5])], inc=(sd_sp, 16))
            em("sync", lambda e, t=t: e.dma_start(out=pa_d[t * 128:(t + 1) * 128, :], in_=pp_v(t)),
               waits=[(sv, qtr_ev[t * 8 + 7])], inc=(sd_pa, 16))
            em("sync", lambda e, t=t: e.dma_start(out=v_d[t * 128:(t + 1) * 128, :], in_=vst),
               waits=[(sv, qtr_ev[t * 8 + 6])], inc=(sd_v, 16))
            tc_dve_done.append(sv.n)
            tc_sa_done.append(sa.n)
            if t + 2 < NTC:
                emit_loads(t + 2)
        sv_P_done = sv.n
        sp_P_done = sp.n

        def aot_sb(c):
            if c < 6:
                return ar2_bf[:, c * 4096:(c + 1) * 4096]
            return ar1_bf[:, (c - 6) * 4096:(c - 5) * 4096]

        qT = lambda ps0: ar1_bf[:, ps0 * 4096: ps0 * 4096 + 4096]
        kT = lambda ps0: ar1_bf[:, 16384 + ps0 * 4096: 16384 + ps0 * 4096 + 4096]

        exp_total = 0
        gqb = 0
        store_done_sv = []
        pair_sp_done = []

        def emit_vload(gp):
            w = [(sd_v, 16 * NTC), (sp, sp_B_done)] if gp < 2 else [(sp, pair_sp_done[gp - 2])]
            for s4 in range(4):
                em("sync", lambda e, c=gp, s4=s4, gp=gp: e.dma_start(
                    out=vbuf(gp)[:, s4 * 1040: s4 * 1040 + 1040].rearrange("p (c2 f) -> p c2 f", f=130),
                    in_=v_d[s4 * 1024:(s4 + 1) * 1024, c * 130:(c + 1) * 130].rearrange(
                        "(c2 p) f -> p c2 f", p=128)),
                    waits=w if s4 == 0 else [], inc=(sd_vl, 16))

        dlv = 0
        if "D" in phases:
            i_ = phases.index("D")
            dlv = int(phases[i_ + 1]) if i_ + 1 < len(phases) and phases[i_ + 1].isdigit() else 9
        DGRP = 2 if dlv >= 1 else 0
        do_exp = "X" not in phases
        if DGRP:
            emit_vload(0)
            emit_vload(1)
        for g in range(DGRP):
            if g == 0:
                w_tr = [(sd_sp, 32 * NTC), (sd_pa, 16 * NTC), (sv, sv_P_done)]
            else:
                w_tr = [(sp, sp.n)]
            for ps0 in range(4):
                em("sync", lambda e, g=g, ps0=ps0: e.dma_start(
                    out=qT(ps0), in_=qr_d[:, g * 512 + ps0 * 128: g * 512 + ps0 * 128 + 128],
                    transpose=True), waits=w_tr if ps0 == 0 else [], inc=(sd_tr, 16))
                em("sync", lambda e, g=g, ps0=ps0: e.dma_start(
                    out=kT(ps0), in_=kr_d[:, g * 512 + ps0 * 128: g * 512 + ps0 * 128 + 128],
                    transpose=True), inc=(sd_tr, 16))
            sd_tr_g = sd_tr.n
            for ps0 in range(4):
                chunk = g * 4 + ps0
                for qb in range(NQB):
                    q0 = qb * QB
                    sa_base = sa.n
                    sc_sp = []
                    abank = ps_b if gqb % 2 == 0 else ps_c

                    def emit_av(kc_av, abank=abank):
                        avw = [(sa, sa_base + kc_av + 1)]
                        if kc_av == 0 and gqb >= 2:
                            avw.append((sv, store_done_sv[gqb - 2]))
                        if kc_av == 0 and qb == 0:
                            avw.append((sd_vl, 64 * (chunk + 1)))
                        for hj in range(2):
                            em("tensor", lambda e, kc=kc_av, hj=hj, gp=chunk, abank=abank: e.matmul(
                                abank[0:65, hj * 512:hj * 512 + 512],
                                vbuf(gp)[:, kc * 130 + hj * VW: kc * 130 + hj * VW + VW],
                                ex_bf[0:128, (kc % 3) * 1024 + hj * 512:(kc % 3) * 1024 + hj * 512 + 512],
                                start=(kc == 0), stop=(kc == NKC - 1)),
                                waits=avw if hj == 0 else [], inc=(sp, 1) if hj == 1 else None)

                    for kc in range(NKC if dlv >= 2 else 0):
                        scw = []
                        if kc == 0 and qb == 0:
                            scw.append((sd_tr, sd_tr_g))
                        if do_exp and kc >= 2:
                            scw.append((sa, sa_base + kc - 1))
                        elif do_exp and gqb > 0:
                            scw.append((sa, sa_base))
                        sbank = ps_a[:, (kc % 2) * 1024:(kc % 2) * 1024 + 1024]
                        em("tensor", lambda e, ps0=ps0, kc=kc, q0=q0, b=sbank: e.matmul(
                            b[:, 0:512], kT(ps0)[0:64, kc * 128:kc * 128 + 128],
                            qT(ps0)[0:64, q0:q0 + QB], start=True, stop=True), waits=scw)
                        em("tensor", lambda e, ps0=ps0, kc=kc, q0=q0, b=sbank: e.matmul(
                            b[:, 512:1024], kT(ps0)[64:128, kc * 128:kc * 128 + 128],
                            qT(ps0)[64:128, q0:q0 + QB], start=True, stop=True,
                            tile_position=(64, 0)), inc=(sp, 1))
                        sc_sp.append(sp.n)
                        ew = [(sp, sc_sp[kc])]
                        if exp_total == 0:
                            ew.append((sg, 3))
                            ew.append((sd_v, 16 * NTC))
                        if do_exp:
                            em("scalar", lp(lambda e, kc=kc, b=sbank: e.activation(
                                ex_bf[:, (kc % 3) * 1024:(kc % 3) * 1024 + 1024], b, AF.Exp,
                                bias=ln_sb[:, 24:25], scale=0.125)), waits=ew, inc=(sa, 1))
                        exp_total += 1
                        if kc >= 1 and dlv >= 3:
                            emit_av(kc - 1)
                    if dlv >= 3:
                        emit_av(NKC - 1)
                    sp_av_last = sp.n
                    if dlv >= 4:
                        cw = [(sp, sp_av_last)]
                        if gqb >= 2:
                            cw.append((sd_ao, 16 * (gqb - 1)))
                        for hj in range(2):
                            em("vector", lp(lambda e, hj=hj, abank=abank, gqb=gqb: e.tensor_copy(
                                as_v(gqb)[hj * 64:hj * 64 + 64, :],
                                abank[0:64, hj * 512:hj * 512 + 512])),
                                waits=cw if hj == 0 else [], inc=(sv, 1))
                        rw = [(sd_rs, 16 * (gqb - 1))] if gqb >= 2 else []
                        em("vector", lp(lambda e, abank=abank, gqb=gqb: e.tensor_copy(
                            rs_v(gqb)[64:65, 0:1024], abank[64:65, 0:1024])), waits=rw, inc=(sv, 1))
                        store_done_sv.append(sv.n)
                        em("sync", lambda e, chunk=chunk, qb=qb, gqb=gqb: e.dma_start(
                            out=ao_d[chunk, :, qb * QB:(qb + 1) * QB], in_=as_v(gqb)),
                            waits=[(sv, sv.n)], inc=(sd_ao, 16))
                        em("sync", lambda e, gqb=gqb: e.dma_start(
                            out=rs_d[gqb:gqb + 1, :], in_=rs_v(gqb)[64:65, 0:1024]),
                            waits=[(sv, sv.n)], inc=(sd_rs, 16))
                    else:
                        store_done_sv.append(sv.n)
                    gqb += 1
                pair_sp_done.append(sp.n)
                if chunk + 2 < 8:
                    emit_vload(chunk + 2)
                if chunk < 6 and "E" in phases and "Y" not in phases:
                    w = [(sd_ao, 16 * NQB * (chunk + 1))]
                    if chunk == 0:
                        w.append((sp, sp_P_done))
                    em("gpsimd", lambda e, c=chunk: e.dma_start(out=aot_sb(c), in_=ao_d[c, :, :]),
                       waits=w, inc=(sd_pe, 16))
        sp_D_done = sp.n

        aot_list = ((list(range(8)) if "Y" in phases else [6, 7]) if "E" in phases else [])
        for c in aot_list:
            w = [(sd_ao, 16 * NQB * (c + 1)), (sp, sp_D_done)]
            em("sync", lambda e, c=c: e.dma_start(out=aot_sb(c), in_=ao_d[c, :, :]), waits=w, inc=(sd_pe, 16))

        rssb = ar1_bf[:, 16384:17408]
        rsrec = ar1_bf[:, 17408:18432]
        rsflat = lambda i: ar1_bf[0:1, 18432:26624]
        presc_sv = {}
        if "E" in phases and "Z" not in phases:
            em("sync", lambda e: e.dma_start(out=rssb[0:64, :], in_=rs_d),
               waits=[(sd_rs, 16 * 64), (sp, sp_D_done)], inc=(sd_rr, 16))
            em("vector", lp(lambda e: e.reciprocal(rsrec[0:64, :], rssb[0:64, :])),
               waits=[(sd_rr, 16)], inc=(sv, 1))
            em("sync", lambda e: e.dma_start(out=rsr_d, in_=rsrec[0:64, :]),
               waits=[(sv, sv.n)], inc=(sd_rr, 16))
            for c in range(8):
                fw = [(sd_rr, 32)] if c == 0 else [(sv, presc_sv[(c - 1, NQB - 1)])]
                em("sync", lambda e, c=c: e.dma_start(
                    out=rsflat(c).rearrange("o (a f) -> o a f", a=8),
                    in_=rsr_d[c * 8:(c + 1) * 8, :].unsqueeze(0)),
                    waits=fw, inc=(sd_rr, 16))
                for qb in range(NQB):
                    i = c * NQB + qb
                    dslot = ps_c[:, (i % 2) * 512:(i % 2) * 512 + 512]
                    bw = [(sd_rr, sd_rr.n)]
                    if i >= 2:
                        bw.append((sv, presc_sv[(c, qb - 2) if qb >= 2 else (c - 1, NQB - 2 + qb)]))
                    em("tensor", lambda e, c=c, qb=qb, dslot=dslot: e.matmul(
                        dslot[0:64, :], onesb[0:1, 0:64], rsflat(c)[0:1, qb * 1024: qb * 1024 + 512],
                        start=True, stop=True), waits=bw)
                    em("tensor", lambda e, c=c, qb=qb, dslot=dslot: e.matmul(
                        dslot[64:128, :], onesb[0:1, 0:64],
                        rsflat(c)[0:1, qb * 1024 + 512: qb * 1024 + 1024],
                        start=True, stop=True, tile_position=(0, 64)), inc=(sp, 1))
                    em("vector", lp(lambda e, c=c, qb=qb, dslot=dslot: e.tensor_tensor(
                        aot_sb(c)[:, qb * QB:(qb + 1) * QB],
                        aot_sb(c)[:, qb * QB:(qb + 1) * QB], dslot, OP.mult)),
                        waits=[(sp, sp.n), (sd_pe, 16 * (c + 1))], inc=(sv, 1))
                    presc_sv[(c, qb)] = sv.n

        ev_e = []

        def emit_pa(t):
            pw = [(sv, ev_e[t - 4])] if t >= 4 else [(sd_pa, 16 * NTC), (sp, sp_D_done)]
            em("sync", lambda e, t=t: e.dma_start(
                out=pa_e(t), in_=pa_d[t * 128:(t + 1) * 128, :]), waits=pw, inc=(sd_pe, 16))

        if "E" in phases:
            for t in range(4):
                emit_pa(t)
        for t in range(NTC if "E" in phases else 0):
            bank = ps_a[:, (t % 2) * 1024:(t % 2) * 1024 + 1024]
            w0 = [(sd_pe, 16 * 8 + 16 * (t + 1))]
            if t >= 2:
                w0.append((sv, ev_e[t - 2]))
            if "Z" not in phases:
                w0.append((sv, presc_sv[(7, t // 4)]))
            for c in range(8):
                em("tensor", lambda e, c=c, t=t, b=bank: e.matmul(
                    b[:, 0:512], aot_sb(c)[:, t * 128:(t + 1) * 128], wo_t(c, 0),
                    start=(c == 0), stop=False), waits=w0 if c == 0 else [])
                em("tensor", lambda e, c=c, t=t, b=bank: e.matmul(
                    b[:, 512:1024], aot_sb(c)[:, t * 128:(t + 1) * 128], wo_t(c, 1),
                    start=(c == 0), stop=(c == 7)), inc=(sp, 1) if c == 7 else None)
            ttw = [(sp, sp.n)]
            if t >= 2:
                ttw.append((sd_out, 16 * (t - 1)))
            em("vector", lambda e, t=t, b=bank: e.tensor_tensor(ou_e(t), b, pa_e(t), OP.add),
               waits=ttw, inc=(sv, 1))
            ev_e.append(sv.n)
            em("sync", lambda e, t=t: e.dma_start(out=out_d[t * 128:(t + 1) * 128, :], in_=ou_e(t)),
               waits=[(sv, sv.n)], inc=(sd_out, 16))
            if t + 4 < NTC:
                emit_pa(t + 4)
        if "E" in phases:
            em("sync", lambda e: None, waits=[(sd_out, sd_out.n)])

        nc._prog_debug = (prog, sems)
        block = stk.enter_context(nc.Block())
        sections = {"sync": block.sync, "vector": block.vector, "scalar": block.scalar,
                    "tensor": block.tensor, "gpsimd": block.gpsimd}

        def emitter(name):
            def body(eng):
                for (waits, fn, inc) in prog[name]:
                    for (sem, val) in waits:
                        if val > 0:
                            eng.wait_ge(sem.h, val)
                    inst = fn(eng)
                    if inc is not None and inst is not None:
                        inst.then_inc(inc[0].h, inc[1])
            return body

        for name, section in sections.items():
            section(emitter(name))
    return nc


def kernel(L0_attn_x, L0_attn_wq, L0_attn_bq, L0_attn_wk, L0_attn_bk,
           L0_attn_wv, L0_attn_bv, L0_attn_wo, L0_attn_bo,
           L0_ffn_w_freq_real, L0_ffn_w_freq_imag, L0_ffn_bias):
    f32 = np.float32
    bf16 = ml_dtypes.bfloat16
    x = np.asarray(L0_attn_x, f32)
    wq = np.asarray(L0_attn_wq, f32); bq = np.asarray(L0_attn_bq, f32)
    wk = np.asarray(L0_attn_wk, f32); bk = np.asarray(L0_attn_bk, f32)
    wv = np.asarray(L0_attn_wv, f32); bv = np.asarray(L0_attn_bv, f32)
    wo = np.asarray(L0_attn_wo, f32); bo = np.asarray(L0_attn_bo, f32)
    wr = np.asarray(L0_ffn_w_freq_real, f32)
    wi = np.asarray(L0_ffn_w_freq_imag, f32)
    bffn = np.asarray(L0_ffn_bias, f32)

    idx = (np.outer(np.arange(E, dtype=np.int64), np.arange(E, dtype=np.int64)) % E)
    ang = (2.0 * np.pi / E) * idx.astype(np.float64)
    cmat = np.cos(ang)
    smat = np.sin(ang)
    ch = cmat.astype(bf16)
    cl = (cmat - ch.astype(np.float64)).astype(bf16)
    sh = smat.astype(bf16)
    sl = (smat - sh.astype(np.float64)).astype(bf16)
    inv_freq = 1.0 / 10000.0 ** (np.arange(0, HD, 2, dtype=np.float64) / HD)
    freqs = np.outer(np.arange(S, dtype=np.float64), inv_freq)
    cos64 = np.repeat(np.cos(freqs), 2, axis=1)
    sin64 = np.repeat(np.sin(freqs), 2, axis=1)
    sin64[:, :32] *= -1.0
    cost = np.ascontiguousarray(np.tile(cos64, (1, H)).astype(f32))
    sint = np.ascontiguousarray(np.tile(sin64, (1, H)).astype(f32))
    ident = np.eye(128, dtype=f32)

    common = {
        "wqT": np.ascontiguousarray(wq.T), "wkT": np.ascontiguousarray(wk.T),
        "wvT": np.ascontiguousarray(wv.T), "woT": np.ascontiguousarray(wo.T),
        "bq": bq.reshape(1, E), "bk": bk.reshape(1, E), "bv": bv.reshape(1, E),
        "bo": bo.reshape(1, E), "bffn": bffn.reshape(1, E),
        "wfr": np.ascontiguousarray(wr), "wfi": np.ascontiguousarray(wi),
        "ch": ch, "cl": cl, "sh": sh, "sl": sl,
        "cost": cost, "sint": sint, "ident": ident,
    }
    in_maps = []
    for c in range(NCORES):
        m = dict(common)
        m["x"] = np.ascontiguousarray(x[c])
        in_maps.append(m)

    nc = build_nc()
    res = run_bass_kernel_spmd(nc, in_maps, core_ids=list(range(NCORES)))
    return np.stack([np.asarray(r["out"], f32) for r in res.results], axis=0)
